# version 11
# baseline (speedup 1.0000x reference)
import numpy as np
import ml_dtypes
import concourse.bass as bass
import concourse.mybir as mybir
from concourse.bass_utils import run_bass_kernel_spmd
from contextlib import ExitStack

F32 = mybir.dt.float32
BF16 = mybir.dt.bfloat16
ALU = mybir.AluOpType
AF = mybir.ActivationFunctionType
AX = mybir.AxisListType

B, T, D = 4, 2048, 2048
DH, NG, NH = 128, 4, 16
I = 8192
TQ = 1024
NKV = 16
MULT = float(DH) ** (-0.25)
SQ2_127 = float(np.float32(np.sqrt(2.0)) * np.float32(127.0))

ENGINES = ("pe", "act", "dve", "pool", "sync")

C_FOLDS = [0, 1, 2, 3, 4, 5, 6, 7]
C_NEGQ, C_ONE, C_ZERO, C_MULT, C_NEG1 = 8, 9, 10, 11, 12

S_HG, S_NHG, S_PHG = 0, 1, 2
S_HU, S_NHU, S_PHU = 3, 4, 5
S_HD, S_NHD, S_PHD = 6, 7, 8
S_K2, S_K2_127, S_LAMD = 9, 10, 11


class _Op:
    __slots__ = ("eng", "fn", "deps", "gid", "need_inc", "sem_val", "is_dma",
                 "tag", "lane")

    def __init__(self, eng, fn, deps, gid, is_dma, tag):
        self.eng, self.fn, self.deps, self.gid = eng, fn, deps, gid
        self.need_inc, self.sem_val, self.is_dma, self.tag = False, None, is_dma, tag
        self.lane = None


NLANES = 8


class Sched:
    def __init__(self):
        self.ops = {e: [] for e in ENGINES}
        self.all = []

    def add(self, eng, fn, deps=(), dma=False, tag=""):
        deps = [d for d in deps if d is not None]
        for d in deps:
            assert isinstance(d, _Op), f"bad dep for {tag}"
        op = _Op(eng, fn, deps, len(self.all), dma, tag)
        self.ops[eng].append(op)
        self.all.append(op)
        return op

    def emit(self, nc):
        for op in self.all:
            for d in op.deps:
                if (d.eng != op.eng or d.is_dma or op.is_dma
                        or d.eng in ("act", "pool", "dve")):
                    d.need_inc = True
        counters = {e: 0 for e in ENGINES}
        dma_cnt = {}
        dma_keys = []
        for e in ENGINES:
            for op in self.ops[e]:
                if op.is_dma:
                    op.need_inc = True
                    nla = dma_cnt.setdefault(e, [0] * NLANES)
                    tot = sum(nla)
                    op.lane = tot % NLANES
                    nla[op.lane] += 1
                    op.sem_val = 16 * nla[op.lane]
                    if (e, op.lane) not in dma_keys:
                        dma_keys.append((e, op.lane))
                elif op.need_inc:
                    counters[e] += 1
                    op.sem_val = counters[e]
        finals = {}
        for e in ENGINES:
            comp = [op for op in self.ops[e] if not op.is_dma]
            if comp:
                op = comp[-1]
                if not op.need_inc:
                    op.need_inc = True
                    counters[e] += 1
                    op.sem_val = counters[e]
                finals[e] = op.sem_val
        dma_finals = {}
        for e, nla in dma_cnt.items():
            for ln in range(NLANES):
                if nla[ln]:
                    dma_finals[(e, ln)] = 16 * nla[ln]

        def key_of(op):
            return (op.eng, op.lane) if op.is_dma else op.eng

        with ExitStack() as ctx:
            EE = ctx.enter_context
            sems = {e: EE(nc.semaphore(f"s_{e}")) for e in ENGINES}
            for k in dma_keys:
                sems[k] = EE(nc.semaphore(f"d_{k[0]}_{k[1]}"))
            block = EE(nc.Block())
            eng_attr = {"pe": "tensor", "act": "scalar", "dve": "vector",
                        "pool": "gpsimd", "sync": "sync"}

            def mk(e):
                ops = self.ops[e]

                def body(h):
                    waited = {}
                    def wait(k, v):
                        if waited.get(k, 0) < v:
                            h.wait_ge(sems[k], v)
                            waited[k] = v
                    for op in ops:
                        need = {}
                        for d in op.deps:
                            k = key_of(d)
                            if k == e and not d.is_dma and e == "pe":
                                continue
                            need[k] = max(need.get(k, 0), d.sem_val)
                        if op.is_dma and op.sem_val > 16:
                            need[key_of(op)] = max(need.get(key_of(op), 0),
                                                   op.sem_val - 16)
                        for k in sorted(need, key=str):
                            wait(k, need[k])
                        try:
                            inst = op.fn(h)
                        except Exception as exc:
                            raise RuntimeError(
                                f"emit failed at op {op.tag}: {exc}") from exc
                        if op.need_inc:
                            assert inst is not None, f"{op.tag}: inc-op returned None"
                            inst.then_inc(sems[key_of(op)], 16 if op.is_dma else 1)
                    if e == "pe":
                        for oe in ENGINES:
                            if oe != e and oe in finals:
                                wait(oe, finals[oe])
                        for k, v in dma_finals.items():
                            wait(k, v)
                return body

            for e in ENGINES:
                getattr(block, eng_attr[e])(mk(e))
        return nc


def _hadamard(n):
    h = np.array([[1.0]])
    while h.shape[0] < n:
        h = np.block([[h, h], [h, -h]])
    return h


def _host_consts():
    cst = np.zeros((128, 32), np.float32)
    vals = dict(zip(C_FOLDS, [-64.0, -32.0, -16.0, -8.0, -4.0, -2.0, -1.0, -0.5]))
    vals.update({C_NEGQ: -0.25, C_ONE: 1.0, C_ZERO: 0.0, C_MULT: MULT,
                 C_NEG1: -1.0})
    for k, v in vals.items():
        cst[:, k] = np.float32(v)
    ones = np.ones((128, 128), np.float32)
    i128f = np.eye(128, dtype=np.float32)
    h128 = np.asarray((_hadamard(128) * 2.0 ** -6).astype(ml_dtypes.bfloat16))
    kk = np.arange(128)
    diagmask = np.where(kk[None, :] < kk[:, None], -1e30, 0.0).astype(np.float32)
    pos = np.arange(T, dtype=np.float64)[:, None]
    dd = np.arange(64, dtype=np.float64)[None, :]
    rad = pos * np.power(10000.0, -dd / 64.0)
    ropeC = np.concatenate([np.cos(rad), np.cos(rad)], 1).astype(np.float32)
    ropeS = np.concatenate([-np.sin(rad), np.sin(rad)], 1).astype(np.float32)
    return cst, ones, i128f, h128, diagmask, ropeC, ropeS


def _parity_layout(par):
    if par == 0:
        kv = list(range(16))
        qt = [0, 1, 2, 3, 12, 13, 14, 15]
        dead = np.zeros((2, 16), np.float32)
        dead[0, 4:8] = -1e30
    else:
        kv = [4, 5, 6, 7, 8, 9, 10, 11, 0, 1, 2, 3, 8, 9, 10, 11]
        qt = [4, 5, 6, 7, 8, 9, 10, 11]
        dead = np.zeros((2, 16), np.float32)
        dead[1, 8:12] = -1e30
    return kv, qt, dead


def _slots(qt):
    out = []
    fs = [8, 9, 10, 11] if qt == 0 else list(range(12))
    cs = [0, 1, 2, 3] if qt == 0 else [12, 13, 14, 15]
    for j, s in enumerate(cs):
        out.append(("C", s, 128 * j, 512 - 128 * j))
    for s in fs:
        out.append(("F", s, 0, 512))
    return out


def build_program(debug=()):
    nc = bass.Bass("TRN2", target_bir_lowering=False)
    g = {}

    def di(name, shape, dt=F32):
        g[name] = nc.dram_tensor(name, list(shape), dt, kind="ExternalInput").ap()

    di("x_kv", (NKV * 128, D)); di("x_q", (TQ, D))
    di("ropeC_kv", (NKV * 128, 128)); di("ropeS_kv", (NKV * 128, 128))
    di("ropeC_q", (TQ, 128)); di("ropeS_q", (TQ, 128))
    di("deadrow", (1, 32))
    di("wq", (D, D)); di("wk", (D, 512)); di("wv", (D, 512)); di("wo", (D, D))
    di("wgT", (D, I)); di("wuT", (D, I)); di("wdT", (I, D))
    di("wmod", (D, 6 * D))
    di("c_vec", (1, D)); di("gq_row", (1, D)); di("gk_row", (1, 512))
    di("cstab", (128, 32)); di("ones", (128, 128)); di("i128f", (128, 128))
    di("h128", (128, 128), BF16); di("diagmask", (128, 128))
    g["out"] = nc.dram_tensor("out", [TQ, D], F32, kind="ExternalOutput").ap()
    for nm, shape, dt in (("x1_hbm", (TQ, D), F32), ("part_hbm", (TQ, D), F32),
                          ("z_hbm", (I, TQ), BF16), ("dn2_hbm", (TQ, D), BF16),
                          ("sbcb_hbm", (256, D), BF16),
                          ("gb_hbm", (128, D), BF16)):
        g[nm] = nc.dram_tensor(nm, list(shape), dt, kind="Internal").ap()
    dbg = {}
    for name, shape, dt in debug:
        g["dbg_" + name] = nc.dram_tensor("dbg_" + name, list(shape), dt,
                                          kind="ExternalOutput").ap()
        dbg[name] = g["dbg_" + name]

    sch = Sched()
    with ExitStack() as octx:
        _prog(nc, g, sch, octx.enter_context, dbg)
        sch.emit(nc)
    return nc


def _prog(nc, g, sch, EO, dbg):
    add = sch.add

    def sb_(E, name, p, f, dt=BF16):
        return E(nc.sbuf_tensor(name, [p, f], dt))

    def ps_(E, name, p, f):
        return E(nc.psum_tensor(name, [p, f], F32))

    def dma(eng, out, in_, deps=(), tag=""):
        return add(eng, lambda h, out=out, in_=in_: h.dma_start(out=out, in_=in_),
                   dma=True, deps=deps, tag=tag)

    def dmaT(out, in_, deps=(), tag=""):
        return add("sync", lambda h, out=out, in_=in_: h.dma_start_transpose(out, in_),
                   dma=True, deps=deps, tag=tag)

    def mm(out, lhsT, rhs, start, stop, deps=(), tag=""):
        return add("pe", lambda h, o=out, l=lhsT, r=rhs, s1=start, s2=stop:
                   h.matmul(o, l, r, start=s1, stop=s2), deps=deps, tag=tag)

    def act(out, in_, func, bias=None, scale=None, accum=None, deps=(), tag=""):
        def f(h, out=out, in_=in_, func=func, bias=bias, scale=scale, accum=accum):
            kw = {}
            if bias is not None:
                kw["bias"] = bias
            if scale is not None:
                kw["scale"] = scale
            if accum is not None:
                kw["accum_out"] = accum
            return h.activation(out, in_, func, **kw)
        return add("act", f, deps=deps, tag=tag)

    def tt(out, a, b, op, deps=(), tag=""):
        return add("dve", lambda h, o=out, a=a, b=b, op=op:
                   h.tensor_tensor(o, a, b, op), deps=deps, tag=tag)

    def tcp(out, in_, deps=(), tag=""):
        return add("dve", lambda h, o=out, i=in_: h.tensor_copy(o, i), deps=deps, tag=tag)

    def ts1(out, in_, s1, op0, deps=(), tag=""):
        return add("dve", lambda h, o=out, i=in_, s1=s1, op0=op0:
                   h.tensor_single_scalar(o, i, s1, op0), deps=deps, tag=tag)

    def ts2(out, in_, s1, s2, op0, op1, deps=(), tag=""):
        return add("dve", lambda h, o=out, i=in_, s1=s1, s2=s2, op0=op0, op1=op1:
                   h.tensor_scalar(o, i, s1, s2, op0, op1), deps=deps, tag=tag)

    def stt(out, in0, s, in1, op0, op1, deps=(), tag=""):
        return add("dve", lambda h, o=out, i0=in0, s=s, i1=in1, op0=op0, op1=op1:
                   h.scalar_tensor_tensor(o, i0, s, i1, op0, op1), deps=deps, tag=tag)

    def recip(out, in_, deps=(), tag=""):
        return add("dve", lambda h, o=out, i=in_: h.reciprocal(o, i), deps=deps, tag=tag)

    def redmax(out, in_, deps=(), tag=""):
        return add("dve", lambda h, o=out, i=in_: h.reduce_max(o, i, axis=AX.X),
                   deps=deps, tag=tag)

    cst = sb_(EO, "cst", 128, 32, F32)
    ones = sb_(EO, "ones_t", 128, 128, F32)
    onesb = sb_(EO, "onesb_t", 128, 4, BF16)
    i128f = sb_(EO, "i128f_t", 128, 128, F32)
    h128 = sb_(EO, "h128_t", 128, 128, BF16)
    dmask = sb_(EO, "dmask_t", 128, 128, F32)
    deadr = sb_(EO, "dead_t", 1, 32, F32)
    i128b = sb_(EO, "i128b", 128, 128, BF16)
    arA = sb_(EO, "arA", 128, 4096, F32)
    arB = sb_(EO, "arB", 128, 4096, F32)
    scl = sb_(EO, "scl", 128, 16, F32)
    wacc = sb_(EO, "wacc", 128, 8, F32)
    ppb = [None, None]
    rowsc = sb_(EO, "rowsc", 1, 64, F32)
    sprow = sb_(EO, "sprow", 1, 1024, F32)
    arAb = arA[:].bitcast(BF16)
    arBb = arB[:].bitcast(BF16)
    aru = {"A": None, "B": None}

    def cc(col):
        return cst[:, col:col + 1]

    evC = [dma("sync", dst[:], g[nm], tag=f"ld_{nm}")
           for nm, dst in (("cstab", cst), ("ones", ones), ("i128f", i128f),
                           ("h128", h128), ("diagmask", dmask), ("deadrow", deadr))]
    ev_onesb = add("dve", lambda h: h.memset(onesb[:], 1.0), tag="onesb")
    ev_ib = ts1(i128b[:], i128f[:], 1.0, ALU.mult, deps=evC, tag="i128b")

    pp_list = []
    for nm, rows in (("wgT", D), ("wuT", D), ("wdT", I)):
        src3 = g[nm].rearrange("(n p) c -> n p c", p=128)
        width = g[nm].shape[1]
        for i in range(rows // 128):
            for hcol in range(width // 1024):
                pp_list.append((nm, src3[i, :, 1024 * hcol:1024 * (hcol + 1)]))
    WCOL = {"wgT": 0, "wuT": 1, "wdT": 2}
    ev_pp_zero = add("dve", lambda h: h.memset(wacc[:, 0:5], 0.0), tag="pp_zero")
    pp = {"i": 0, "use": [None, None], "acc": {k: ev_pp_zero for k in WCOL},
          "last": {}, "scr_rd": [None, None]}

    def pp_step(n=1):
        for _ in range(n):
            i = pp["i"]
            if i >= len(pp_list):
                return
            pp["i"] = i + 1
            nm, src = pp_list[i]
            lane = i % 2
            buf = ppb[lane]
            scr = 3 + lane
            ld = dma("sync", buf[:], src, deps=[pp["use"][lane]], tag=f"pp_ld{i}")
            a = act(buf[:], buf[:], AF.Abs, accum=wacc[:, scr:scr + 1],
                    deps=[ld, pp["scr_rd"][lane]], tag=f"pp_abs{i}")
            pp["use"][lane] = a
            wc = WCOL[nm]
            acc = tt(wacc[:, wc:wc + 1], wacc[:, wc:wc + 1], wacc[:, scr:scr + 1],
                     ALU.add, deps=[a, pp["acc"][nm]], tag=f"pp_acc{i}")
            pp["acc"][nm] = acc
            pp["scr_rd"][lane] = acc
            pp["last"][nm] = acc

    def rsqrt_chain(pfx, v_ap, w, scr, deps):
        s0 = scr[:, 0 * w:1 * w]; t1 = scr[:, 1 * w:2 * w]
        t2 = scr[:, 2 * w:3 * w]; s1 = scr[:, 3 * w:4 * w]
        e0 = act(s0, v_ap, AF.Sqrt, deps=deps, tag=pfx + "sq")
        e1 = recip(t1, s0, deps=[e0], tag=pfx + "r0")
        e2 = tt(t2, v_ap, t1, ALU.mult, deps=[e1], tag=pfx + "vm")
        e3 = tt(s1, s0, t2, ALU.add, deps=[e2], tag=pfx + "ad")
        e4 = ts1(s1, s1, 0.5, ALU.mult, deps=[e3], tag=pfx + "hf")
        return recip(v_ap, s1, deps=[e4], tag=pfx + "rf")

    c1 = ExitStack()
    E1 = c1.enter_context
    saT = sb_(E1, "saT", 128, D, BF16)
    caT = sb_(E1, "caT", 128, D, BF16)
    gaT = sb_(E1, "gaT", 128, D, BF16)
    kfm = [sb_(E1, f"kfm{i}", 128, 2048, BF16) for i in range(4)]
    vtm = sb_(E1, "vtm", 128, 8192, BF16)
    qfm = sb_(E1, "qfm", 128, 16384, BF16)
    stats = sb_(E1, "stats", 128, 32, F32)
    msq = sb_(E1, "msq", 128, 16, F32)
    knrm = sb_(E1, "knrm", 128, 64, F32)
    cmax = sb_(E1, "cmax", 128, 32, F32)
    mgb = sb_(E1, "mgb", 128, 4, F32)
    small_ps = ps_(E1, "small_ps", 128, 512)
    pe_ps = [ps_(E1, f"pe_ps{j}", 128, 512) for j in range(4)]
    pe_ps2 = [ps_(E1, f"pe_ps2_{j}", 128, 512) for j in range(3)]
    ppb[0] = sb_(E1, "ppb0", 128, 1024, F32)
    ppb[1] = sb_(E1, "ppb1", 128, 1024, F32)
    gqmB = sb_(E1, "gqmB", 128, D, BF16)
    gkmB = sb_(E1, "gkmB", 128, 512, BF16)
    dmp = sb_(E1, "dmp", 128, 2048, BF16)

    cB = ExitStack()
    EB = cB.enter_context
    crow_s = sb_(EB, "crow", 1, D, F32)
    silc = sb_(EB, "silc", 1, D, F32)
    scol = sb_(EB, "scol", 128, 16, F32)
    repc = sb_(EB, "repc", 128, D, BF16)
    garw = sb_(EB, "garw", 128, D, F32)
    gbrw = sb_(EB, "gbrw", 128, D, F32)
    sbT0 = sb_(EB, "sbT0", 128, D, BF16)
    cbT0 = sb_(EB, "cbT0", 128, D, BF16)
    gbT0 = sb_(EB, "gbT0", 128, D, BF16)
    grow = sb_(EB, "grow", 1, D, F32)
    gkrow = sb_(EB, "gkrow", 1, 512, F32)
    bscr = sb_(EB, "bscr", 128, 4096, F32)

    ev_c = dma("sync", crow_s[:], g["c_vec"], tag="ld_c")
    ev_silc = act(silc[:], crow_s[:], AF.Silu, deps=[ev_c], tag="silu_c")
    ev_scol_mm = []
    for kc in range(16):
        ev_scol_mm.append(mm(small_ps[:, kc:kc + 1], silc[0:1, 128 * kc:128 * (kc + 1)],
                             ones[0:1, 0:1], True, True,
                             deps=[ev_silc] + evC, tag=f"scolmm{kc}"))
    ev_scol = act(scol[:], small_ps[:, 0:16], AF.Identity, deps=ev_scol_mm, tag="scol_ev")
    ev_repc = []
    for kc in range(16):
        ev_repc.append(act(repc[:, 128 * kc:128 * (kc + 1)], ones[:], AF.Identity,
                           scale=scol[:, kc:kc + 1], deps=[ev_scol], tag=f"repc{kc}"))

    ev_gq = dma("sync", grow[:], g["gq_row"], tag="ld_gq")
    ev_gk = dma("sync", gkrow[:], g["gk_row"], tag="ld_gk")
    ev_gqm0 = ts1(grow[:], grow[:], MULT, ALU.mult, deps=[ev_gq], tag="gqmul")
    ev_gkm0 = ts1(gkrow[:], gkrow[:], MULT, ALU.mult, deps=[ev_gk], tag="gkmul")
    prev = ev_scol
    for j in range(4):
        e1 = mm(small_ps[:, 0:512], ones[0:1, :], grow[:, 512 * j:512 * (j + 1)],
                True, True, deps=[ev_gqm0, prev] + evC, tag=f"gqm{j}")
        prev = act(gqmB[:, 512 * j:512 * (j + 1)], small_ps[:, 0:512], AF.Identity,
                   deps=[e1], tag=f"gqme{j}")
    e1 = mm(small_ps[:, 0:512], ones[0:1, :], gkrow[:], True, True,
            deps=[ev_gkm0, prev], tag="gkm")
    ev_gkm = act(gkmB[:], small_ps[:, 0:512], AF.Identity, deps=[e1], tag="gkme")

    wmod3 = g["wmod"].rearrange("(kc p) c -> p kc c", p=128)
    raw_names = ["sa", "ca", "ga", "sb", "cb", "gb"]
    prev_mm_h = [None, None]
    ev_mod_ev = {}
    for grp in range(24):
        half = grp % 2
        dstb = arAb if half == 0 else arBb
        ld = add("pool", lambda h, grp=grp, dstb=dstb: h.dma_start(
            out=dstb[0:128, 0:8192].rearrange("p (kc c) -> p kc c", c=512),
            in_=wmod3[:, :, 512 * grp:512 * (grp + 1)]),
            dma=True, deps=[prev_mm_h[half]] + evC, tag=f"wmld{grp}")
        last = None
        for kc in range(16):
            dep = [ld]
            if kc == 0:
                dep += ev_repc
                if grp >= 2:
                    dep += [ev_mod_ev[grp - 2]]
            last = mm(pe_ps[half][:], repc[:, 128 * kc:128 * (kc + 1)],
                      dstb[0:128, 512 * kc:512 * (kc + 1)], kc == 0, kc == 15,
                      deps=dep, tag=f"wmmm{grp}_{kc}")
        prev_mm_h[half] = last
        nm = raw_names[grp // 4]
        off = (grp % 4) * 512
        if nm in ("ca", "cb"):
            tgt = caT if nm == "ca" else cbT0
            ev = act(tgt[:, off:off + 512], pe_ps[half][:], AF.Identity,
                     bias=cc(C_ONE), deps=[last], tag=f"wmev{grp}")
        elif nm in ("sa", "sb"):
            tgt = saT if nm == "sa" else sbT0
            ev = act(tgt[:, off:off + 512], pe_ps[half][:], AF.Identity,
                     deps=[last], tag=f"wmev{grp}")
        else:
            raw = garw if nm == "ga" else gbrw
            ev = act(raw[:, off:off + 512], pe_ps[half][:], AF.Identity,
                     deps=[last], tag=f"wmev{grp}")
        ev_mod_ev[grp] = ev
    ev_sa = [ev_mod_ev[j] for j in range(0, 4)]
    ev_ca = [ev_mod_ev[j] for j in range(4, 8)]
    ev_ga = [ev_mod_ev[j] for j in range(8, 12)]
    ev_sbcb = [ev_mod_ev[j] for j in range(12, 20)]
    ev_gb = [ev_mod_ev[j] for j in range(20, 24)]
    aru["A"] = prev_mm_h[0]
    aru["B"] = prev_mm_h[1]

    def gate_tile(pfx, raw, out_t, deps):
        s1 = ts1(bscr[:, 0:2048], raw[:], 0.0, ALU.is_ge, deps=deps, tag=pfx + "ge")
        s2 = ts2(bscr[:, 0:2048], bscr[:, 0:2048], 2.0, -1.0, ALU.mult, ALU.add,
                 deps=[s1], tag=pfx + "sg")
        a1 = act(bscr[:, 2048:4096], raw[:], AF.Abs, deps=deps, tag=pfx + "ab")
        m1 = ts1(bscr[:, 2048:4096], bscr[:, 2048:4096], 0.25, ALU.max,
                 deps=[a1], tag=pfx + "mx")
        return tt(out_t[:], bscr[:, 0:2048], bscr[:, 2048:4096], ALU.mult,
                  deps=[s2, m1], tag=pfx + "ml")

    ev_gaT = gate_tile("ga", garw, gaT, ev_ga)
    ev_gbT = gate_tile("gb", gbrw, gbT0, ev_gb + [ev_gaT])
    ev_st_sb = dma("sync", g["sbcb_hbm"][0:128, :], sbT0[:], deps=ev_sbcb, tag="st_sb")
    ev_st_cb = dma("sync", g["sbcb_hbm"][128:256, :], cbT0[:], deps=ev_sbcb, tag="st_cb")
    ev_st_gb = dma("sync", g["gb_hbm"], gbT0[:], deps=[ev_gbT], tag="st_gb")
    ev_phaseB = [ev_st_sb, ev_st_cb, ev_st_gb, ev_gaT] + ev_sa + ev_ca + ev_repc \
        + [ev_gkm, ev_scol]
    cB.close()
    pp_step(4)

    cK = ExitStack()
    EK = cK.enter_context
    xt = sb_(EK, "xt", 128, D, F32)
    ht = [sb_(EK, "ht0", 128, D, BF16)]
    hfmR = [sb_(EK, "hfmR0", 128, 2048, BF16)]
    hfmQb = sb_(EK, "hfmQb", 128, 16384, BF16)
    qs = sb_(EK, "qs", 128, D, F32)
    qs2 = sb_(EK, "qs2", 128, D, F32)
    qbs = sb_(EK, "qbs", 128, D, BF16)
    ks = sb_(EK, "ks", 128, 512, F32)
    kbs = sb_(EK, "kbs", 128, 512, BF16)
    ropC = sb_(EK, "ropC", 128, 128, F32)
    ropS = sb_(EK, "ropS", 128, 128, F32)

    def ln_tile(pfx, x_ap, out_ht, ca_t, sa_t, ev_x, extra, ca_deps):
        d0 = act(dmp[:], x_ap, AF.Identity, accum=stats[:, 0:1],
                 deps=[ev_x] + extra, tag=pfx + "sx")
        d1 = act(dmp[:], x_ap, AF.Square, accum=stats[:, 1:2], deps=[d0], tag=pfx + "sx2")
        m1 = ts1(stats[:, 2:3], stats[:, 0:1], 1.0 / 2048, ALU.mult, deps=[d0], tag=pfx + "m")
        m2 = ts1(stats[:, 3:4], stats[:, 1:2], 1.0 / 2048, ALU.mult, deps=[d1], tag=pfx + "e2")
        m3 = tt(stats[:, 4:5], stats[:, 2:3], stats[:, 2:3], ALU.mult, deps=[m1], tag=pfx + "mm")
        m4 = tt(stats[:, 4:5], stats[:, 3:4], stats[:, 4:5], ALU.subtract,
                deps=[m2, m3], tag=pfx + "var")
        m5 = ts1(stats[:, 4:5], stats[:, 4:5], 1e-5, ALU.add, deps=[m4], tag=pfx + "veps")
        mr = rsqrt_chain(pfx, stats[:, 4:5], 1, stats[:, 6:10], [m5])
        m6 = ts1(stats[:, 5:6], stats[:, 2:3], -1.0, ALU.mult, deps=[m1], tag=pfx + "negm")
        a0 = ts2(out_ht[:], x_ap, stats[:, 5:6], stats[:, 4:5], ALU.add, ALU.mult,
                 deps=[mr, m6], tag=pfx + "app")
        a1 = tt(out_ht[:], out_ht[:], ca_t[:], ALU.mult, deps=[a0] + ca_deps, tag=pfx + "ca")
        return tt(out_ht[:], out_ht[:], sa_t[:], ALU.add, deps=[a1], tag=pfx + "sa")

    xkv3 = g["x_kv"].rearrange("(s p) c -> s p c", p=128)
    rc3 = g["ropeC_kv"].rearrange("(s p) c -> s p c", p=128)
    rs3 = g["ropeS_kv"].rearrange("(s p) c -> s p c", p=128)
    wk3 = g["wk"].rearrange("(kc p) c -> p kc c", p=128)
    wv3 = g["wv"].rearrange("(kc p) c -> p kc c", p=128)
    ev_wk = add("pool", lambda h: h.dma_start(
        out=arAb[0:128, 0:8192].rearrange("p (kc c) -> p kc c", c=512), in_=wk3),
        dma=True, deps=[aru["A"]], tag="ld_wk")
    ev_wv = add("pool", lambda h: h.dma_start(
        out=arBb[0:128, 0:8192].rearrange("p (kc c) -> p kc c", c=512), in_=wv3),
        dma=True, deps=[aru["B"]], tag="ld_wv")

    use = {"xt": None, "ht": [list(ev_phaseB)], "hfmR": [None], "rope": None,
           "ks": None, "kbs": list(ev_phaseB), "ln": None, "qbs": [],
           "kT_ps": [], "qT_ps": [[], []]}
    kvtp = {0: ev_mod_ev[22], 1: ev_mod_ev[23]}
    ev_kT_all = []
    ev_v_last = None
    ev_knrm_all = []
    lk = lv = None

    for s in range(NKV):
        par = 0
        ph0 = ev_phaseB if s == 0 else []
        evx = dma("sync", xt[:], xkv3[s], deps=[use["xt"]] + ph0, tag=f"kvx{s}")
        evrc = dma("sync", ropC[:], rc3[s], deps=[use["rope"]] + ph0, tag=f"kvrc{s}")
        evrs = dma("sync", ropS[:], rs3[s], deps=[use["rope"]] + ph0, tag=f"kvrs{s}")
        ev_h = ln_tile(f"kl{s}_", xt[:], ht[par], caT, saT, evx,
                       list(use["ht"][par]) + [use["ln"]], ev_ca + ev_sa)
        use["ln"] = ev_h
        use["xt"] = ev_h
        evT = [None] * 16
        for gq_ in range(4):
            bank = pe_ps[gq_ % 2]
            tms = []
            for k2 in range(4):
                fc = 4 * gq_ + k2
                tms.append(mm(bank[:, 128 * k2:128 * (k2 + 1)],
                              ht[par][:, 128 * fc:128 * (fc + 1)], i128b[:],
                              True, True, deps=[ev_h, ev_ib, kvtp.get(gq_ % 2)],
                              tag=f"kvT{s}_{gq_}_{k2}"))
            evg = tcp(hfmR[par][:, 512 * gq_:512 * (gq_ + 1)], bank[:],
                      deps=tms + [use["hfmR"][par]] + ph0, tag=f"kvTe{s}_{gq_}")
            kvtp[gq_ % 2] = evg
            for k2 in range(4):
                evT[4 * gq_ + k2] = evg
        use["ht"][par] = [evT[0], evT[4], evT[8], evT[12]]
        for fc in range(16):
            lk = mm(pe_ps[2][:], hfmR[par][:, 128 * fc:128 * (fc + 1)],
                    arAb[0:128, 512 * fc:512 * (fc + 1)], fc == 0, fc == 15,
                    deps=[evT[fc], ev_wk] + ([use["ks"]] if fc == 0 else []),
                    tag=f"kmm{s}_{fc}")
        for fc in range(16):
            lv = mm(pe_ps[3][:], hfmR[par][:, 128 * fc:128 * (fc + 1)],
                    arBb[0:128, 512 * fc:512 * (fc + 1)], fc == 0, fc == 15,
                    deps=[evT[fc], ev_wv] + ([ev_v_last] if fc == 0 else []),
                    tag=f"vmm{s}_{fc}")
        use["hfmR"][par] = lv
        ev_v_last = act(vtm[:, 512 * s:512 * (s + 1)], pe_ps[3][:], AF.Identity,
                        deps=[lv], tag=f"vev{s}")
        ek0 = act(ks[:], pe_ps[2][:], AF.Identity, deps=[lk] + ph0, tag=f"kev{s}")
        eacc = []
        for gi in range(4):
            eacc.append(act(dmp[:, 128 * gi:128 * (gi + 1)], ks[:, 128 * gi:128 * (gi + 1)],
                            AF.Square, accum=stats[:, 10 + gi:11 + gi],
                            deps=[ek0], tag=f"krms{s}_{gi}"))
        em0 = ts2(stats[:, 10:14], stats[:, 10:14], 1.0 / 128, 1e-6, ALU.mult, ALU.add,
                  deps=eacc, tag=f"kms{s}")
        emr = rsqrt_chain(f"kr{s}_", stats[:, 10:14], 4, stats[:, 14:30], [em0])
        ek1 = []
        for gi in range(4):
            ek1.append(ts1(ks[:, 128 * gi:128 * (gi + 1)], ks[:, 128 * gi:128 * (gi + 1)],
                           stats[:, 10 + gi:11 + gi], ALU.mult, deps=[emr],
                           tag=f"kno{s}_{gi}"))
        ek2 = tt(ks[:], ks[:], gkmB[:], ALU.mult, deps=ek1 + [ev_gkm], tag=f"kgn{s}")
        e_t1 = tt(qs[:, 0:512].rearrange("p (gr d) -> p gr d", d=128),
                  ks[:].rearrange("p (gr d) -> p gr d", d=128),
                  ropC[:].unsqueeze(1).broadcast_to([128, 4, 128]),
                  ALU.mult, deps=[ek2, evrc] + ph0, tag=f"krp1_{s}")
        e_rope = [e_t1]
        for gi in range(4):
            b0 = 128 * gi
            e_rope.append(tt(qs2[:, b0:b0 + 64], ks[:, b0 + 64:b0 + 128], ropS[:, 0:64],
                             ALU.mult, deps=[ek2, evrs] + ph0, tag=f"krp2a{s}_{gi}"))
            e_rope.append(tt(qs2[:, b0 + 64:b0 + 128], ks[:, b0:b0 + 64], ropS[:, 64:128],
                             ALU.mult, deps=[ek2, evrs] + ph0, tag=f"krp2b{s}_{gi}"))
        e_k = tt(kbs[:], qs[:, 0:512], qs2[:, 0:512], ALU.add,
                 deps=e_rope + list(use["kbs"]), tag=f"krpo{s}")
        use["ks"] = e_k
        use["rope"] = e_k
        ekn = []
        for gi in range(4):
            ekn.append(act(dmp[:, 128 * (4 + gi):128 * (5 + gi)],
                           kbs[:, 128 * gi:128 * (gi + 1)], AF.Square,
                           accum=knrm[:, 4 * s + gi:4 * s + gi + 1],
                           deps=[e_k], tag=f"knrm{s}_{gi}"))
        ev_knrm_all += ekn
        ktms = []
        for gi in range(4):
            ktms.append(mm(pe_ps2[0][:, 128 * gi:128 * (gi + 1)],
                           kbs[:, 128 * gi:128 * (gi + 1)], i128b[:], True, True,
                           deps=[e_k, ev_ib] + ekn + use["kT_ps"],
                           tag=f"kT{s}_{gi}"))
        ekT = []
        for gi in range(4):
            ekT.append(tcp(kfm[gi][:, 128 * s:128 * (s + 1)],
                           pe_ps2[0][:, 128 * gi:128 * (gi + 1)], deps=ktms,
                           tag=f"kTe{s}_{gi}"))
        use["kT_ps"] = ekT
        ev_kT_all += ekT
        use["kbs"] = ekT
        pp_step(3)
    aru["A"] = lk
    aru["B"] = lv
    use["qT_ps"][0] = use["kT_ps"]

    e1 = add("dve", lambda h: h.reduce_max(
        stats[:, 10:14], knrm[:].rearrange("p (s gr) -> p gr s", gr=4), axis=AX.X),
        deps=ev_knrm_all, tag="mg_red")
    e2 = mm(small_ps[0:4, 0:128], stats[:, 10:14], i128f[:], True, True,
            deps=[e1, ev_scol_mm[-1]], tag="mg_tp")
    e3 = redmax(stats[0:4, 14:15], small_ps[0:4, 0:128], deps=[e2], tag="mg_red2")
    e3b = act(stats[0:4, 14:15], stats[0:4, 14:15], AF.Sqrt, deps=[e3], tag="mg_sqrt")
    e4 = mm(small_ps[0:1, 128:132], stats[0:4, 14:15], i128f[0:4, 0:4], True, True,
            deps=[e3b], tag="mg_row")
    e5 = act(rowsc[:, 0:4], small_ps[0:1, 128:132], AF.Identity, deps=[e4], tag="mg_rowev")
    e6 = mm(small_ps[:, 0:4], ones[0:1, :], rowsc[0:1, 0:4], True, True,
            deps=[e5], tag="mg_bc")
    ev_mgb = act(mgb[:], small_ps[:, 0:4], AF.Identity, deps=[e6], tag="mg_ev")

    xq3 = g["x_q"].rearrange("(s p) c -> s p c", p=128)
    rcq3 = g["ropeC_q"].rearrange("(s p) c -> s p c", p=128)
    rsq3 = g["ropeS_q"].rearrange("(s p) c -> s p c", p=128)
    wq4 = g["wq"].rearrange("(kc p) c -> p kc c", p=128)
    ev_cmax_last = None
    ev_qT_all = []
    ev_qmm = {}

    for oh in range(2):
        hh0 = 8 * oh
        depA = [aru["A"]] if oh == 0 else [ev_qmm[(0, 7)]]
        depB = [aru["B"]] if oh == 0 else [ev_qmm[(0, 7)]]
        evA = add("pool", lambda h, oh=oh: h.dma_start(
            out=arAb[0:128, 0:8192].rearrange("p (kc c) -> p kc c", c=1024),
            in_=wq4[:, 0:8, 1024 * oh:1024 * (oh + 1)]),
            dma=True, deps=depA, tag=f"ldwqA{oh}")
        evB = add("pool", lambda h, oh=oh: h.dma_start(
            out=arBb[0:128, 0:8192].rearrange("p (kc c) -> p kc c", c=1024),
            in_=wq4[:, 8:16, 1024 * oh:1024 * (oh + 1)]),
            dma=True, deps=depB, tag=f"ldwqB{oh}")
        for tq in range(8):
            qt = tq // 4
            par = 0
            if oh == 0:
                evx = dma("sync", xt[:], xq3[tq], deps=[use["xt"]], tag=f"qx{tq}")
                ev_h = ln_tile(f"ql{tq}_", xt[:], ht[par], caT, saT, evx,
                               list(use["ht"][par]) + [use["ln"]], [])
                use["ln"] = ev_h
                evT = [None] * 16
                for gq_ in range(4):
                    bank = pe_ps[gq_ % 2]
                    tms = []
                    for k2 in range(4):
                        fc = 4 * gq_ + k2
                        tms.append(mm(bank[:, 128 * k2:128 * (k2 + 1)],
                                      ht[par][:, 128 * fc:128 * (fc + 1)], i128b[:],
                                      True, True,
                                      deps=[ev_h, ev_ib, kvtp.get(gq_ % 2)],
                                      tag=f"qT{tq}_{gq_}_{k2}"))
                    evg = tcp(hfmQb[:, 2048 * tq + 512 * gq_:
                                    2048 * tq + 512 * (gq_ + 1)], bank[:],
                              deps=tms, tag=f"qTe{tq}_{gq_}")
                    kvtp[gq_ % 2] = evg
                    for k2 in range(4):
                        evT[4 * gq_ + k2] = evg
                use["ht"][par] = [evT[0], evT[4], evT[8], evT[12]]
                dep_h = evT
            else:
                dep_h = [ev_qmm[(0, tq)]]
            evrc = dma("sync", ropC[:], rcq3[tq], deps=[use["rope"]], tag=f"qrc{oh}_{tq}")
            evrs = dma("sync", ropS[:], rsq3[tq], deps=[use["rope"]], tag=f"qrs{oh}_{tq}")
            lq = [None, None]
            for fc in range(16):
                arb = arAb if fc < 8 else arBb
                base = 1024 * (fc % 8)
                for j in range(2):
                    dep = [dep_h[fc] if oh == 0 else dep_h[0],
                           evA if fc < 8 else evB]
                    if fc == 0 and j == 0:
                        dep.append(ev_qmm.get((oh, tq - 1)))
                        if oh == 0 and tq == 0:
                            dep.append(use["ks"])
                    lq[j] = mm(pe_ps[2 + j][:],
                               hfmQb[:, 2048 * tq + 128 * fc:2048 * tq + 128 * (fc + 1)],
                               arb[0:128, base + 512 * j:base + 512 * (j + 1)],
                               fc == 0, fc == 15, deps=dep, tag=f"qmm{oh}_{tq}_{fc}_{j}")
            ev_q0 = act(qs[:, 1024 * oh:1024 * oh + 512], pe_ps[2][:], AF.Identity,
                        deps=[lq[0]], tag=f"qev{oh}_{tq}_0")
            ev_q1 = act(qs[:, 1024 * oh + 512:1024 * (oh + 1)], pe_ps[3][:], AF.Identity,
                        deps=[lq[1]], tag=f"qev{oh}_{tq}_1")
            ev_qmm[(oh, tq)] = ev_q1
            ea = []
            for hh in range(8):
                b0 = 1024 * oh + 128 * hh
                ea.append(act(dmp[:, 128 * hh:128 * (hh + 1)], qs[:, b0:b0 + 128],
                              AF.Square, accum=msq[:, hh0 + hh:hh0 + hh + 1],
                              deps=[ev_q0, ev_q1], tag=f"qrms{oh}_{tq}_{hh}"))
            em0 = ts2(msq[:, hh0:hh0 + 8], msq[:, hh0:hh0 + 8], 1.0 / 128, 1e-6,
                      ALU.mult, ALU.add, deps=ea, tag=f"qms{oh}_{tq}")
            emr = rsqrt_chain(f"qr{oh}_{tq}_", msq[:, hh0:hh0 + 8], 8, stats[:, 0:32], [em0])
            eb = []
            for hh in range(8):
                b0 = 1024 * oh + 128 * hh
                eb.append(ts1(qs[:, b0:b0 + 128], qs[:, b0:b0 + 128],
                              msq[:, hh0 + hh:hh0 + hh + 1], ALU.mult,
                              deps=[emr], tag=f"qno{oh}_{tq}_{hh}"))
            ec = tt(qs[:, 1024 * oh:1024 * (oh + 1)], qs[:, 1024 * oh:1024 * (oh + 1)],
                    gqmB[:, 1024 * oh:1024 * (oh + 1)], ALU.mult, deps=eb,
                    tag=f"qgn{oh}_{tq}")
            ed = []
            for hh in range(8):
                b0 = 1024 * oh + 128 * hh
                ed.append(act(dmp[:, 128 * (8 + hh):128 * (9 + hh)], qs[:, b0:b0 + 128],
                              AF.Square, accum=msq[:, hh0 + hh:hh0 + hh + 1],
                              deps=[ec], tag=f"qn2{oh}_{tq}_{hh}"))
            e_t1 = tt(qs2[:, 1024 * oh:1024 * (oh + 1)].rearrange("p (hh d) -> p hh d", d=128),
                      qs[:, 1024 * oh:1024 * (oh + 1)].rearrange("p (hh d) -> p hh d", d=128),
                      ropC[:].unsqueeze(1).broadcast_to([128, 8, 128]),
                      ALU.mult, deps=[evrc] + ed, tag=f"qrp1_{oh}_{tq}")
            e_rope = [e_t1]
            for hh in range(8):
                b0 = 1024 * oh + 128 * hh
                e_rope.append(tt(xt[:, b0:b0 + 64], qs[:, b0 + 64:b0 + 128],
                                 ropS[:, 0:64], ALU.mult,
                                 deps=[evrs, use["ln"]] + ed, tag=f"qrp2a{oh}_{tq}_{hh}"))
                e_rope.append(tt(xt[:, b0 + 64:b0 + 128], qs[:, b0:b0 + 64],
                                 ropS[:, 64:128], ALU.mult,
                                 deps=[evrs] + ed, tag=f"qrp2b{oh}_{tq}_{hh}"))
            e_q = tt(qbs[:, 1024 * oh:1024 * (oh + 1)], qs2[:, 1024 * oh:1024 * (oh + 1)],
                     xt[:, 1024 * oh:1024 * (oh + 1)], ALU.add,
                     deps=e_rope + list(use["qbs"]), tag=f"qrpo{oh}_{tq}")
            use["rope"] = e_q
            use["xt"] = e_q
            es = act(msq[:, hh0:hh0 + 8], msq[:, hh0:hh0 + 8], AF.Sqrt,
                     deps=ed + [ev_mgb], tag=f"csq{oh}_{tq}")
            eM = tt(msq[:, hh0:hh0 + 8].rearrange("p (a b) -> p a b", b=4),
                    msq[:, hh0:hh0 + 8].rearrange("p (a b) -> p a b", b=4),
                    mgb[:, 2 * oh:2 * oh + 2].unsqueeze(2).broadcast_to([128, 2, 4]),
                    ALU.mult, deps=[es], tag=f"cmg{oh}_{tq}")
            if tq % 4 == 0:
                ecm = tcp(cmax[:, 16 * qt + hh0:16 * qt + hh0 + 8], msq[:, hh0:hh0 + 8],
                          deps=[eM, ev_cmax_last], tag=f"cmax0_{oh}_{tq}")
            else:
                ecm = tt(cmax[:, 16 * qt + hh0:16 * qt + hh0 + 8],
                         cmax[:, 16 * qt + hh0:16 * qt + hh0 + 8],
                         msq[:, hh0:hh0 + 8], ALU.max,
                         deps=[eM, ev_cmax_last], tag=f"cmaxm_{oh}_{tq}")
            ev_cmax_last = ecm
            eqT = []
            for hb in range(2):
                bank = pe_ps2[hb]
                tms = []
                for k2 in range(4):
                    hh = 4 * hb + k2
                    b0 = 1024 * oh + 128 * hh
                    tms.append(mm(bank[:, 128 * k2:128 * (k2 + 1)],
                                  qbs[:, b0:b0 + 128], i128b[:], True, True,
                                  deps=[e_q, ev_ib] + use["qT_ps"][hb],
                                  tag=f"qfT{oh}_{tq}_{hh}"))
                evs_ = []
                for k2 in range(4):
                    hh = 4 * hb + k2
                    hg = hh0 + hh
                    evs_.append(tcp(qfm[:, 1024 * hg + 512 * qt + 128 * (tq % 4):
                                        1024 * hg + 512 * qt + 128 * (tq % 4) + 128],
                                    bank[:, 128 * k2:128 * (k2 + 1)], deps=tms,
                                    tag=f"qfTe{oh}_{tq}_{hh}"))
                use["qT_ps"][hb] = evs_
                eqT += evs_
            ev_qT_all += eqT
            use["qbs"] = eqT
            pp_step(3)
    aru["A"] = ev_qmm[(1, 7)]
    aru["B"] = ev_qmm[(1, 7)]
    cK.close()

    cA = ExitStack()
    EA = cA.enter_context
    ofm = sb_(EA, "ofm", 128, 8192, BF16)
    cbias = sb_(EA, "cbias", 128, 512, F32)
    cbrow = sb_(EA, "cbrow", 1, 256, F32)
    pbuf = [sb_(EA, f"pbuf{i}", 128, 512, BF16) for i in range(2)]
    wscr = sb_(EA, "wscr", 128, D, F32)
    wscr2 = sb_(EA, "wscr2", 128, D, F32)
    rbS = sb_(EA, "rbS", 128, 512, F32)
    xtw = sb_(EA, "xtw", 128, D, F32)
    att_sc = [pe_ps[0], pe_ps[1]]
    att_o = pe_ps[2]
    att_sm = pe_ps[3]

    en1 = mm(small_ps[0:32, 0:128], cmax[:], i128f[:], True, True,
             deps=[ev_cmax_last, e6], tag="nc_tp")
    en2 = redmax(stats[0:32, 0:1], small_ps[0:32, 0:128], deps=[en1], tag="nc_red")
    en3 = mm(small_ps[0:1, 128:160], stats[0:32, 0:1], i128f[0:32, 0:32], True, True,
             deps=[en2], tag="nc_row")
    en4 = act(rowsc[:, 32:64], small_ps[0:1, 128:160], AF.Identity, deps=[en3],
              tag="nc_rowev")
    ev_cb = []
    for qt in range(2):
        a_ap = rowsc[0:1, 32 + 16 * qt:48 + 16 * qt].unsqueeze(2).broadcast_to([1, 16, 16])
        b_ap = deadr[0:1, 16 * qt:16 * (qt + 1)].unsqueeze(1).broadcast_to([1, 16, 16])
        ecr = add("dve", lambda h, a_ap=a_ap, b_ap=b_ap: h.scalar_tensor_tensor(
            cbrow[0:1, 0:256].rearrange("r (a b) -> r a b", b=16),
            a_ap, -1.0, b_ap, ALU.mult, ALU.add),
            deps=[en4] + evC + ([ev_cb[-1]] if ev_cb else []), tag=f"cbrow{qt}")
        eb1 = mm(small_ps[:, 0:256], ones[0:1, :], cbrow[0:1, 0:256], True, True,
                 deps=[ecr, en3], tag=f"cbmm{qt}")
        ev_cb.append(act(cbias[:, 256 * qt:256 * (qt + 1)], small_ps[:, 0:256],
                         AF.Identity, deps=[eb1], tag=f"cbev{qt}"))

    wo4 = g["wo"].rearrange("(dc p) c -> p dc c", p=128)
    ev_x1_st = []
    ev_wo_ev = {}
    ev_oe = {}
    for qt in range(2):
        slots = _slots(qt)
        for hg in range(16):
            gi = hg // 4
            last_i = len(slots) - 1
            ev_exp = [None, None]
            ev_pv = None
            ev_smm = None
            exs = [None] * len(slots)
            pvs = [None] * len(slots)
            sms = [None] * len(slots)
            for si in range(len(slots) + 1):
                if si < len(slots):
                    kind, s, qoff, ncols = slots[si]
                    par = si % 2
                    dep_sc = [ev_exp[par]]
                    if (qt, hg) == (0, 0) and si <= 1:
                        dep_sc += ev_kT_all + ev_qT_all
                    if si <= 1:
                        dep_sc += [ev_oe.get((qt, hg - 1)), ev_cb[-1]]
                        if hg == 0 and ev_x1_st:
                            dep_sc.append(ev_x1_st[-1])
                    sc = mm(att_sc[par][:, 0:ncols], kfm[gi][:, 128 * s:128 * (s + 1)],
                            qfm[:, 1024 * hg + 512 * qt + qoff:
                                1024 * hg + 512 * qt + qoff + ncols],
                            True, True, deps=dep_sc, tag=f"sc{qt}_{hg}_{si}")
                    dep_e = [sc]
                    if kind == "C":
                        mska = tt(att_sc[par][:, 0:128], att_sc[par][:, 0:128],
                                  dmask[:], ALU.add, deps=[sc],
                                  tag=f"msk{qt}_{hg}_{si}")
                        dep_e = [mska]
                    bcol = cbias[:, 256 * qt + 16 * hg + si:256 * qt + 16 * hg + si + 1]
                    if si >= 2:
                        dep_e += [pvs[si - 2], sms[si - 2]]
                    ex = act(pbuf[par][:, 0:ncols], att_sc[par][:, 0:ncols], AF.Exp,
                             bias=bcol, deps=dep_e + [ev_cb[qt]],
                             tag=f"ex{qt}_{hg}_{si}")
                    ev_exp[par] = ex
                    exs[si] = ex
                if si >= 1:
                    sj = si - 1
                    kind1, s1, qoff1, ncols1 = slots[sj]
                    par1 = sj % 2
                    ev_pv = mm(att_o[:, qoff1:qoff1 + ncols1],
                               vtm[:, 512 * s1 + 128 * gi:512 * s1 + 128 * (gi + 1)],
                               pbuf[par1][:, 0:ncols1], sj == 0, sj == last_i,
                               deps=[exs[sj], ev_v_last, ev_pv],
                               tag=f"pv{qt}_{hg}_{sj}")
                    pvs[sj] = ev_pv
                    ev_smm = mm(att_sm[0:1, qoff1:qoff1 + ncols1], onesb[:, 0:1],
                                pbuf[par1][:, 0:ncols1], sj == 0, sj == last_i,
                                deps=[exs[sj], ev_onesb, ev_smm],
                                tag=f"sm{qt}_{hg}_{sj}")
                    sms[sj] = ev_smm
            err = recip(sprow[0:1, 0:512], att_sm[0:1, 0:512], deps=[ev_smm],
                        tag=f"rr{qt}_{hg}")
            ebc = mm(small_ps[:, 0:512], ones[0:1, :], sprow[0:1, 0:512], True, True,
                     deps=[err, ev_oe.get((qt, hg - 1)),
                           (ev_cb[-1] if (qt, hg) == (0, 0) else None)],
                     tag=f"rb{qt}_{hg}")
            erbe = act(rbS[:], small_ps[:, 0:512], AF.Identity,
                       deps=[ebc, ev_oe.get((qt, hg - 1))], tag=f"rbe{qt}_{hg}")
            ev_oe[(qt, hg)] = tt(ofm[:, 512 * hg:512 * (hg + 1)], att_o[:],
                                 rbS[:], ALU.mult,
                                 deps=[erbe, ev_pv], tag=f"oev{qt}_{hg}")
            ev_att_done = [ev_oe[(qt, hg)], err, erbe, ev_exp[0], ev_exp[1],
                           ev_pv, ev_smm]
            pp_step(4)

        pe8 = pe_ps + pe_ps2 + [small_ps]
        for pr in range(2):
            lwA = None
            lwB = None
            evac_pair = []
            for hp in range(2):
                eldA = add("pool", lambda h, hp=hp: h.dma_start(
                    out=arAb[0:128, 0:8192].rearrange("p (dc c) -> p dc c", c=2048),
                    in_=wo4[:, 8 * hp:8 * hp + 4, :]), dma=True,
                    deps=[aru["A"]], tag=f"ldwoA{qt}_{pr}_{hp}")
                eldB = add("pool", lambda h, hp=hp: h.dma_start(
                    out=arBb[0:128, 0:8192].rearrange("p (dc c) -> p dc c", c=2048),
                    in_=wo4[:, 8 * hp + 4:8 * hp + 8, :]), dma=True,
                    deps=[aru["B"]], tag=f"ldwoB{qt}_{pr}_{hp}")
                for ti in range(2):
                    tloc = 2 * pr + ti
                    for dc8 in range(8):
                        dcg = 8 * hp + dc8
                        arb = arAb if dc8 < 4 else arBb
                        base = 2048 * (dc8 % 4)
                        for j in range(4):
                            dep = [eldA if dc8 < 4 else eldB, ev_oe[(qt, dcg)]]
                            if hp == 0 and dc8 == 0:
                                dep.append(ev_wo_ev.get((4 * ti + j,)))
                                if pr == 0:
                                    dep += ev_att_done
                            lw = mm(pe8[4 * ti + j][:],
                                    ofm[:, 512 * dcg + 128 * tloc:
                                        512 * dcg + 128 * tloc + 128],
                                    arb[0:128, base + 512 * j:base + 512 * (j + 1)],
                                    hp == 0 and dc8 == 0, hp == 1 and dc8 == 7,
                                    deps=dep, tag=f"wo{qt}_{pr}_{hp}_{tloc}_{dc8}_{j}")
                            if dc8 < 4:
                                lwA = lw
                            else:
                                lwB = lw
                aru["A"] = lwA
                aru["B"] = lwB
            for ti in range(2):
                tloc = 2 * pr + ti
                tq = 4 * qt + tloc
                wdst = wscr if ti == 0 else wscr2
                evs = [act(wdst[:, 512 * j:512 * (j + 1)], pe8[4 * ti + j][:],
                           AF.Identity, deps=[lwA, lwB,
                           (ev_x1_st[-1] if ev_x1_st else None)],
                           tag=f"wpe{qt}_{pr}_{ti}_{j}") for j in range(4)]
                for j in range(4):
                    ev_wo_ev[(4 * ti + j,)] = evs[j]
                eg = tt(wdst[:], wdst[:], gaT[:], ALU.mult,
                        deps=evs + [ev_gaT], tag=f"wga{qt}_{pr}_{ti}")
                exl = dma("sync", xtw[:], xq3[tq],
                          deps=[eg, (ev_x1_st[-1] if ev_x1_st else None)],
                          tag=f"wxl{qt}_{pr}_{ti}")
                ex1 = tt(xtw[:], xtw[:], wdst[:], ALU.add, deps=[exl, eg],
                         tag=f"wx1{qt}_{pr}_{ti}")
                ev_x1_st.append(dma("sync", g["x1_hbm"][128 * tq:128 * (tq + 1), :],
                                    xtw[:], deps=[ex1], tag=f"x1st{qt}_{pr}_{ti}"))
    while pp["i"] < len(pp_list):
        pp_step(8)
    cA.close()
    c1.close()

    def scal_bcast(pfx, wc, colH, colNH, colPH, denom, deps):
        e1 = mm(small_ps[0:1, 0:1], wacc[:, wc:wc + 1], ones[:, 0:1], True, True,
                deps=deps + [en3], tag=pfx + "sum")
        e2 = act(rowsc[:, 8 + wc:9 + wc], small_ps[0:1, 0:1], AF.Identity,
                 deps=[e1], tag=pfx + "row")
        e3 = mm(small_ps[:, 0:1], ones[0:1, :], rowsc[0:1, 8 + wc:9 + wc], True, True,
                deps=[e2], tag=pfx + "bc")
        e4 = act(scl[:, colH:colH + 1], small_ps[:, 0:1], AF.Identity, deps=[e3],
                 tag=pfx + "ev")
        e5 = ts1(scl[:, colH:colH + 1], scl[:, colH:colH + 1], 0.5 / denom, ALU.mult,
                 deps=[e4], tag=pfx + "h")
        e6 = ts1(scl[:, colNH:colNH + 1], scl[:, colH:colH + 1], -1.0, ALU.mult,
                 deps=[e5], tag=pfx + "nh")
        e7 = tcp(scl[:, colPH:colPH + 1], scl[:, colH:colH + 1], deps=[e5], tag=pfx + "ph")
        return [e5, e6, e7]

    ev_sg = scal_bcast("sg", 0, S_HG, S_NHG, S_PHG, D * I,
                       [pp["last"]["wgT"]] + [ev_wo_ev[(j,)] for j in range(8)])
    ev_su = scal_bcast("su", 1, S_HU, S_NHU, S_PHU, D * I, [pp["last"]["wuT"]] + ev_sg)
    ev_sd = scal_bcast("sd", 2, S_HD, S_NHD, S_PHD, D * I, [pp["last"]["wdT"]] + ev_su)
    ek2a = ts1(scl[:, S_K2:S_K2 + 1], scl[:, S_HU:S_HU + 1],
               float(np.float32(1.0 / np.sqrt(2.0))),
               ALU.mult, deps=ev_su, tag="k2")
    ek2b = ts1(scl[:, S_K2_127:S_K2_127 + 1], scl[:, S_K2:S_K2 + 1], 127.0, ALU.mult,
               deps=[ek2a], tag="k2b")
    elam = ts1(scl[:, S_LAMD:S_LAMD + 1], scl[:, S_HD:S_HD + 1], 2.0, ALU.mult,
               deps=ev_sd, tag="lamd")

    cxq = ExitStack()
    Exq = cxq.enter_context
    xq = [sb_(Exq, f"xq{fc}", 128, 1024, BF16) for fc in range(16)]
    c2 = ExitStack()
    E2 = c2.enter_context
    sbT = sb_(E2, "sbT", 128, D, BF16)
    cbT = sb_(E2, "cbT", 128, D, BF16)
    xt2 = [sb_(E2, "xt2a", 128, D, F32), sb_(E2, "xt2b", 128, D, F32)]
    h2t = [sb_(E2, f"h2t{i}", 128, D, BF16) for i in range(2)]
    h2fm = [sb_(E2, f"h2fm{i}", 128, 2048, BF16) for i in range(2)]
    f1 = [sb_(E2, f"f1_{fc}", 128, 1024, F32) for fc in range(16)]
    Mq = sb_(E2, "Mq", 128, 1024, F32)
    tscr = sb_(E2, "tscr", 128, 1024, F32)
    z1s = sb_(E2, "z1s", 128, 1024, F32)
    z2s = sb_(E2, "z2s", 128, 1024, BF16)
    sBt = sb_(E2, "sBt", 128, 1024, F32)
    rBt = sb_(E2, "rBt", 128, 1024, F32)
    st2l = [sb_(E2, "st2a", 128, 32, F32), sb_(E2, "st2b", 128, 32, F32)]
    dmpl = [sb_(E2, "dmp2a", 128, 2048, BF16), sb_(E2, "dmp2b", 128, 2048, BF16)]
    f_ps = [ps_(E2, "f_ps0", 128, 512), ps_(E2, "f_ps1", 128, 512)]
    tp_ps = [ps_(E2, "tp_ps0", 128, 512), ps_(E2, "tp_ps1", 128, 512)]
    sm2_ps = ps_(E2, "sm2_ps", 128, 512)

    ev_sbT = dma("sync", sbT[:], g["sbcb_hbm"][0:128, :], deps=[ev_st_sb], tag="ld_sbT")
    ev_cbT = dma("sync", cbT[:], g["sbcb_hbm"][128:256, :], deps=[ev_st_cb], tag="ld_cbT")

    def ln2_tile(pfx, x_ap, out_ht, ev_x, extra, lane=0):
        st2 = st2l[lane]
        dmp = dmpl[lane]
        d0 = act(dmp[:], x_ap, AF.Identity, accum=st2[:, 0:1],
                 deps=[ev_x] + extra, tag=pfx + "sx")
        d1 = act(dmp[:], x_ap, AF.Square, accum=st2[:, 1:2], deps=[d0], tag=pfx + "sx2")
        m1 = ts1(st2[:, 2:3], st2[:, 0:1], 1.0 / 2048, ALU.mult, deps=[d0], tag=pfx + "m")
        m2 = ts1(st2[:, 3:4], st2[:, 1:2], 1.0 / 2048, ALU.mult, deps=[d1], tag=pfx + "e2")
        m3 = tt(st2[:, 4:5], st2[:, 2:3], st2[:, 2:3], ALU.mult, deps=[m1], tag=pfx + "mm")
        m4 = tt(st2[:, 4:5], st2[:, 3:4], st2[:, 4:5], ALU.subtract, deps=[m2, m3],
                tag=pfx + "var")
        m5 = ts1(st2[:, 4:5], st2[:, 4:5], 1e-5, ALU.add, deps=[m4], tag=pfx + "veps")
        mr = rsqrt_chain(pfx, st2[:, 4:5], 1, st2[:, 6:10], [m5])
        m6 = ts1(st2[:, 5:6], st2[:, 2:3], -1.0, ALU.mult, deps=[m1], tag=pfx + "negm")
        a0 = ts2(out_ht[:], x_ap, st2[:, 5:6], st2[:, 4:5], ALU.add, ALU.mult,
                 deps=[mr, m6], tag=pfx + "app")
        a1 = tt(out_ht[:], out_ht[:], cbT[:], ALU.mult, deps=[a0, ev_cbT], tag=pfx + "cb")
        return tt(out_ht[:], out_ht[:], sbT[:], ALU.add, deps=[a1, ev_sbT], tag=pfx + "sb")

    use2 = {"xt2": [None, None], "h2t": [None, None], "h2fm": [None, None],
            "ln": [None, None]}
    ev_f1 = {}
    tpuse = {}
    for tq in range(8):
        par = tq % 2
        evx = dma("sync", xt2[par][:], g["x1_hbm"][128 * tq:128 * (tq + 1), :],
                  deps=[use2["xt2"][par], ev_x1_st[tq]], tag=f"l2x{tq}")
        ev_h = ln2_tile(f"l2_{tq}_", xt2[par][:], h2t[par], evx,
                        [use2["h2t"][par], use2["ln"][par]], lane=par)
        use2["ln"][par] = ev_h
        use2["xt2"][par] = ev_h
        evT = [None] * 4
        lastf = None
        for step in range(6):
            if step < 4:
                gq_ = step
                bank = tp_ps[gq_ % 2]
                tms = []
                for k in range(4):
                    fc = 4 * gq_ + k
                    tms.append(mm(bank[:, 128 * k:128 * (k + 1)],
                                  h2t[par][:, 128 * fc:128 * (fc + 1)], i128b[:],
                                  True, True,
                                  deps=[ev_h, ev_ib, tpuse.get(gq_ % 2)],
                                  tag=f"l2T{tq}_{gq_}_{k}"))
                evT[gq_] = tcp(h2fm[par][:, 512 * gq_:512 * (gq_ + 1)], bank[:],
                               deps=tms + [ev_f1.get((tq - 2, 4 * gq_ + k))
                                           for k in range(4)],
                               tag=f"l2Te{tq}_{gq_}")
                tpuse[gq_ % 2] = evT[gq_]
            if step >= 2:
                gq2 = step - 2
                for k in range(4):
                    fc2 = 4 * gq2 + k
                    pj = fc2 % 2
                    ef = mm(f_ps[pj][:, 0:128], h128[:],
                            h2fm[par][:, 128 * fc2:128 * (fc2 + 1)], True, True,
                            deps=[evT[gq2], evC[3], ev_f1.get((tq, fc2 - 2))],
                            tag=f"f1mm{tq}_{fc2}")
                    ev_f1[(tq, fc2)] = act(f1[fc2][:, 128 * tq:128 * (tq + 1)],
                                           f_ps[pj][:, 0:128], AF.Identity,
                                           deps=[ef], tag=f"f1ev{tq}_{fc2}")
                    lastf = ev_f1[(tq, fc2)]
        use2["h2t"][par] = evT[3]
        use2["h2fm"][par] = lastf

    ev_f1_all = list(ev_f1.values())
    ev_mix = {fc: [ev_f1[(tq, fc)] for tq in range(8)] for fc in range(16)}
    for st in range(4):
        hstep = 1 << st
        newev = {}
        for base in range(0, 16, 2 * hstep):
            for off in range(hstep):
                a = base + off
                b = a + hstep
                e1m = tt(f1[a][:], f1[a][:], f1[b][:], ALU.add,
                         deps=ev_mix[a] + ev_mix[b], tag=f"mx1_{st}_{a}")
                e2m = stt(f1[b][:], f1[b][:], -2.0, f1[a][:], ALU.mult, ALU.add,
                          deps=[e1m], tag=f"mx2_{st}_{b}")
                newev[a] = [e2m]
                newev[b] = [e2m]
        ev_mix = newev

    eM = act(Mq[:], f1[0][:], AF.Abs, deps=ev_mix[0], tag="q1_abs0")
    for fc in range(1, 16):
        ea = act(tscr[:], f1[fc][:], AF.Abs, deps=ev_mix[fc] + [eM], tag=f"q1_abs{fc}")
        eM = tt(Mq[:], Mq[:], tscr[:], ALU.max, deps=[ea], tag=f"q1_max{fc}")
    ev_row = []
    for tb in range(8):
        t1 = mm(sm2_ps[:, 0:128], Mq[:, 128 * tb:128 * (tb + 1)], i128f[:], True, True,
                deps=[eM] + ([ev_row[-1]] if ev_row else []), tag=f"q1tp{tb}")
        t2 = redmax(st2l[0][:, 10:11], sm2_ps[:, 0:128], deps=[t1], tag=f"q1rm{tb}")
        t3 = mm(sm2_ps[0:1, 128:256], st2l[0][:, 10:11], i128f[:], True, True,
                deps=[t2], tag=f"q1rw{tb}")
        ev_row.append(act(sprow[:, 128 * tb:128 * (tb + 1)], sm2_ps[0:1, 128:256],
                          AF.Identity, deps=[t3], tag=f"q1re{tb}"))
    ecl = ts2(sprow[0:1, 0:1024], sprow[0:1, 0:1024],
              float(np.float32(np.sqrt(2.0))), 1e-5, ALU.mult, ALU.max,
              deps=ev_row, tag="q1cl")
    erc = recip(z1s[0:1, 0:1024], sprow[0:1, 0:1024], deps=[ecl], tag="q1rc")
    ev_sB = []
    for j in range(2):
        b1 = mm(sm2_ps[:, 0:512], ones[0:1, :], sprow[0:1, 512 * j:512 * (j + 1)],
                True, True, deps=[ecl, en3 if j == 0 else ev_sB[-1],
                                  (ev_row[-1] if j == 0 else None)], tag=f"q1rbm{j}")
        e_rb = act(rBt[:, 512 * j:512 * (j + 1)], sm2_ps[:, 0:512], AF.Identity,
                   deps=[b1], tag=f"q1rbe{j}")
        b2 = mm(sm2_ps[:, 0:512], ones[0:1, :], z1s[0:1, 512 * j:512 * (j + 1)],
                True, True, deps=[erc, e_rb], tag=f"q1sbm{j}")
        e_sb = act(sBt[:, 512 * j:512 * (j + 1)], sm2_ps[:, 0:512], AF.Identity,
                   deps=[b2], tag=f"q1sbe{j}")
        ev_sB += [e_rb, e_sb]
    esc1 = ts1(rBt[:], rBt[:], 1.0 / 127.0, ALU.mult, deps=ev_sB, tag="q1rbs")
    esc2 = ts1(sBt[:], sBt[:], SQ2_127, ALU.mult, deps=ev_sB, tag="q1sbs")

    def pq1(out, a, b, op, deps=(), tag=""):
        return add("pool", lambda h, o=out, a=a, b=b, op=op:
                   h.tensor_tensor(o, a, b, op), deps=deps, tag=tag)

    ev_xq = {}
    uq1 = {"t": None, "z1": None, "z2": None}
    for fc in range(16):
        e_t = pq1(tscr[:], f1[fc][:], sBt[:], ALU.mult,
                  deps=ev_mix[fc] + [esc2, uq1["t"]], tag=f"q1t{fc}")
        ez1 = act(z1s[:], tscr[:], AF.Abs, deps=[e_t, erc, uq1["z1"]],
                  tag=f"q1z1_{fc}")
        for ci, cf in enumerate(C_FOLDS):
            ez1 = act(z1s[:], z1s[:], AF.Abs, bias=cc(cf), deps=[ez1],
                      tag=f"q1f1_{fc}_{ci}")
        d0 = ts1(z2s[:], tscr[:], -0.25, ALU.add, deps=[e_t, uq1["z2"]],
                 tag=f"q1d0_{fc}")
        ez2 = stt(z2s[:], z2s[:], -1.0, z2s[:], ALU.mult, ALU.max, deps=[d0],
                  tag=f"q1da_{fc}")
        for ci in range(8):
            vv = float(-(2.0 ** (6 - ci)) if ci < 7 else -0.5)
            f0 = ts1(z2s[:], z2s[:], vv, ALU.add, deps=[ez2], tag=f"q1df{fc}_{ci}")
            ez2 = stt(z2s[:], z2s[:], -1.0, z2s[:], ALU.mult, ALU.max, deps=[f0],
                      tag=f"q1dg{fc}_{ci}")
        eg = ts1(z2s[:], z2s[:], 0.25, ALU.is_ge, deps=[ez2], tag=f"q1g{fc}")
        esg = ts2(z2s[:], z2s[:], 2.0, -1.0, ALU.mult, ALU.add, deps=[eg], tag=f"q1sg{fc}")
        ed1 = ts2(z1s[:], z1s[:], -1.0, 0.5, ALU.mult, ALU.add, deps=[ez1], tag=f"q1d1{fc}")
        ee = pq1(z1s[:], z1s[:], z2s[:], ALU.mult, deps=[ed1, esg], tag=f"q1e{fc}")
        eq = pq1(tscr[:], tscr[:], z1s[:], ALU.subtract, deps=[ee], tag=f"q1q{fc}")
        ev_xq[fc] = pq1(xq[fc][:], tscr[:], rBt[:], ALU.mult, deps=[eq, esc1],
                        tag=f"q1xq{fc}")
        uq1["t"] = ev_xq[fc]
        uq1["z1"] = eq
        uq1["z2"] = ee
    c2.close()

    c3 = ExitStack()
    E3 = c3.enter_context
    terng = [sb_(E3, f"terng{i}", 128, 2048, BF16) for i in range(2)]
    ternu = [sb_(E3, f"ternu{i}", 128, 2048, BF16) for i in range(2)]
    s2scr = sb_(E3, "s2scr", 128, 2048, BF16)
    sgt = sb_(E3, "sgt", 128, 512, BF16)
    zt = sb_(E3, "zt", 128, 1024, BF16)
    gu_ps = [ps_(E3, f"gu_ps{j}", 128, 512) for j in range(8)]

    wgT3 = g["wgT"].rearrange("(fc p) o -> p fc o", p=128)
    wuT3 = g["wuT"].rearrange("(fc p) o -> p fc o", p=128)
    arq = {0: ("A", arA, arAb), 1: ("B", arB, arBb)}
    ev_tern_use = {"g": [None, None], "u": [None, None]}
    ev_gu = {}
    ev_zst = {}
    ev_sil_rd = {}
    ev_zz_rd = {}
    for c in range(64):
        cpar = c % 2
        nmA, arf, arb_ = arq[cpar]
        elg = dma("sync", arf[:, 0:2048].rearrange("p (fc o) -> p fc o", o=128),
                  wgT3[:, :, 128 * c:128 * (c + 1)],
                  deps=[aru[nmA]], tag=f"gld{c}")
        elu = dma("pool", arf[:, 2048:4096].rearrange("p (fc o) -> p fc o", o=128),
                  wuT3[:, :, 128 * c:128 * (c + 1)],
                  deps=[aru[nmA]], tag=f"uld{c}")
        eg1 = act(terng[cpar][:], arf[:, 0:2048], AF.Sign, bias=scl[:, S_NHG:S_NHG + 1],
                  deps=[elg, ev_tern_use["g"][cpar]] + ev_sg, tag=f"tg1_{c}")
        eg2 = act(s2scr[:], arf[:, 0:2048], AF.Sign, bias=scl[:, S_PHG:S_PHG + 1],
                  deps=[eg1], tag=f"tg2_{c}")
        eg3 = tt(terng[cpar][:], terng[cpar][:], s2scr[:], ALU.add, deps=[eg2],
                 tag=f"tg3_{c}")
        eu1 = act(ternu[cpar][:], arf[:, 2048:4096], AF.Sign, bias=scl[:, S_NHU:S_NHU + 1],
                  deps=[elu, ev_tern_use["u"][cpar], eg2] + ev_su, tag=f"tu1_{c}")
        eu2 = act(s2scr[:], arf[:, 2048:4096], AF.Sign, bias=scl[:, S_PHU:S_PHU + 1],
                  deps=[eu1, eg3], tag=f"tu2_{c}")
        eu3 = tt(ternu[cpar][:], ternu[cpar][:], s2scr[:], ALU.add, deps=[eu2],
                 tag=f"tu3_{c}")
        aru[nmA] = eu2
        bk = 4 * (c % 2)
        lg = [None, None]
        for fc in range(16):
            for j in range(2):
                dep = [eg3, ev_xq[fc]]
                if fc == 0:
                    dep.append(ev_sil_rd.get((c - 2, j)))
                lg[j] = mm(gu_ps[bk + j][:], terng[cpar][:, 128 * fc:128 * (fc + 1)],
                           xq[fc][:, 512 * j:512 * (j + 1)], fc == 0, fc == 15,
                           deps=dep, tag=f"gmm{c}_{fc}_{j}")
        ev_gu[("g", c)] = lg[1]
        lu = [None, None]
        for fc in range(16):
            for j in range(2):
                dep = [eu3, ev_xq[fc]]
                if fc == 0:
                    dep.append(ev_zz_rd.get((c - 2, j)))
                lu[j] = mm(gu_ps[bk + 2 + j][:], ternu[cpar][:, 128 * fc:128 * (fc + 1)],
                           xq[fc][:, 512 * j:512 * (j + 1)], fc == 0, fc == 15,
                           deps=dep, tag=f"umm{c}_{fc}_{j}")
        ev_gu[("u", c)] = lu[1]
        ev_tern_use["g"][cpar] = lg[1]
        ev_tern_use["u"][cpar] = lu[1]
        ezz = []
        for j in range(2):
            es1 = act(sgt[:], gu_ps[bk + j][:], AF.Silu, scale=scl[:, S_HG:S_HG + 1],
                      deps=[lg[j], (ev_zst.get(c - 1)),
                            (ezz[-1] if ezz else None)] + ev_sg, tag=f"sil{c}_{j}")
            ev_sil_rd[(c, j)] = es1
            ez = tt(zt[:, 512 * j:512 * (j + 1)], sgt[:], gu_ps[bk + 2 + j][:],
                    ALU.mult, deps=[es1, lu[j]], tag=f"zmul{c}_{j}")
            ev_zz_rd[(c, j)] = ez
            ezz.append(ez)
        ev_zst[c] = dma("sync", g["z_hbm"][128 * c:128 * (c + 1), :], zt[:],
                        deps=ezz, tag=f"zst{c}")
    c3.close()
    cxq.close()

    czh = ExitStack()
    Ez = czh.enter_context
    zhA = sb_(Ez, "zhA", 128, 64 * 512, BF16)
    zhB = sb_(Ez, "zhB", 128, 64 * 512, BF16)
    c4 = ExitStack()
    E4 = c4.enter_context
    tscr2 = sb_(E4, "tscr2", 128, 1024, F32)
    z1q = sb_(E4, "z1q", 128, 1024, F32)
    z2q = sb_(E4, "z2q", 128, 1024, BF16)
    Mq2 = sb_(E4, "Mq2", 128, 512, F32)
    sB2 = sb_(E4, "sB2", 128, 512, F32)
    rB2 = sb_(E4, "rB2", 128, 512, BF16)
    f2_ps = [ps_(E4, "f2_ps0", 128, 512), ps_(E4, "f2_ps1", 128, 512)]
    c5 = ExitStack()
    E5 = c5.enter_context
    ternd = sb_(E5, "ternd", 128, 8192, BF16)
    dob = sb_(E5, "dob", 128, 512, BF16)
    dtb = sb_(E5, "dtb", 128, 512, BF16)
    dnt = sb_(E5, "dnt", 128, 512, BF16)
    xtd = sb_(E5, "xtd", 128, 512, F32)
    gbT2 = sb_(E5, "gbT2", 128, D, BF16)
    d_ps = [ps_(E5, "d_ps0", 128, 512), ps_(E5, "d_ps1", 128, 512)]
    dT_ps = ps_(E5, "dT_ps", 128, 512)
    d2_ps = ps_(E5, "d2_ps", 128, 512)

    ev_gbL = dma("sync", gbT2[:], g["gb_hbm"], deps=[ev_st_gb], tag="ld_gb2")
    ev_gb2 = add("dve", lambda h: h.tensor_single_scalar(
        gbT2[:], gbT2[:], scl[:, S_LAMD:S_LAMD + 1], ALU.mult),
        deps=[ev_gbL, elam], tag="gb_lam")

    zS = {0: zhA, 1: zhB}
    wdT3 = g["wdT"].rearrange("(fc p) o -> p fc o", p=128)

    def zsl(th_, c):
        return zS[th_][:, 512 * c:512 * (c + 1)]

    def ccb(col):
        return cst[:, col:col + 1].broadcast_to([128, 512])

    def ptt(out, a, b, op, deps=(), tag=""):
        return add("pool", lambda h, o=out, a=a, b=b, op=op:
                   h.tensor_tensor(o, a, b, op), deps=deps, tag=tag)

    use4 = {"tscr": None, "z1q": None, "z2q": None, "Mq2": None, "sB2": None,
            "rB2": None, "sprow": ev_sB + [erc], "dnt": None, "xtd": None,
            "d2": None}
    ev_u = {}
    ev_dn2 = {}

    def phase_load_h128(th_, extra=()):
        evl = {}
        evh = {}
        for c in range(64):
            evl[c] = dma("sync", zsl(th_, c),
                         g["z_hbm"][128 * c:128 * (c + 1), 512 * th_:512 * (th_ + 1)],
                         deps=[ev_zst[c]], tag=f"zl{th_}_{c}")
        for c in range(64):
            pj = c % 2
            em_ = mm(f2_ps[pj][:], h128[:], zsl(th_, c), True, True,
                     deps=[evl[c], evC[3], evh.get(c - 2)]
                     + (list(extra) if c < 2 else []), tag=f"f2m{th_}_{c}")
            evh[c] = act(zsl(th_, c), f2_ps[pj][:], AF.Identity, deps=[em_],
                         tag=f"f2e{th_}_{c}")
        return evh

    def phase_mix(th_, evh):
        last = None
        for st in range(6):
            hstep = 1 << st
            f = 512 * hstep
            vw = zS[th_][:].rearrange("p (g two f) -> p g two f", two=2, f=f)
            av = vw[:, :, 0, :]
            bv = vw[:, :, 1, :]
            dep = [last] if last else list(evh.values())
            e1m = tt(av, av, bv, ALU.add, deps=dep, tag=f"w2a{th_}_{st}")
            last = stt(bv, bv, -2.0, av, ALU.mult, ALU.add, deps=[e1m],
                       tag=f"w2b{th_}_{st}")
        return last

    def phase_amax_scales(th_, ev_mix):
        eM = None
        eN = None
        for p in range(32):
            a = zS[th_][:, 1024 * p:1024 * p + 512]
            b = zS[th_][:, 1024 * p + 512:1024 * (p + 1)]
            e1 = tt(z2q[:, 0:512], a, b, ALU.max,
                    deps=[ev_mix, use4["z2q"], eM], tag=f"q2x{th_}_{p}")
            e3 = tt(z2q[:, 512:1024], a, b, ALU.min, deps=[e1], tag=f"q2n{th_}_{p}")
            if p == 0:
                eM = tcp(Mq2[:], z2q[:, 0:512], deps=[e1, use4["Mq2"]],
                         tag=f"q2M{th_}_{p}")
                eN = tcp(sB2[:], z2q[:, 512:1024], deps=[e3, use4["sB2"]],
                         tag=f"q2N{th_}_{p}")
            else:
                eM = tt(Mq2[:], Mq2[:], z2q[:, 0:512], ALU.max, deps=[e1, eM],
                        tag=f"q2M{th_}_{p}")
                eN = tt(sB2[:], sB2[:], z2q[:, 512:1024], ALU.min, deps=[e3, eN],
                        tag=f"q2N{th_}_{p}")
        em1 = ts1(sB2[:], sB2[:], -1.0, ALU.mult, deps=[eN], tag=f"q2ng{th_}")
        eM = tt(Mq2[:], Mq2[:], sB2[:], ALU.max, deps=[eM, em1], tag=f"q2mm{th_}")
        use4["z2q"] = e1
        ev_r2 = []
        t2 = None
        for tb in range(4):
            t1 = mm(f2_ps[0][:, 0:128], Mq2[:, 128 * tb:128 * (tb + 1)], i128f[:],
                    True, True, deps=[eM, t2] + ([ev_r2[-1]] if ev_r2 else []),
                    tag=f"q2tp{th_}_{tb}")
            t2 = redmax(wacc[:, 6:7], f2_ps[0][:, 0:128], deps=[t1],
                        tag=f"q2rm{th_}_{tb}")
            t3 = mm(f2_ps[0][0:1, 128:256], wacc[:, 6:7], i128f[:], True, True,
                    deps=[t2], tag=f"q2rw{th_}_{tb}")
            ev_r2.append(act(sprow[:, 128 * tb:128 * (tb + 1)], f2_ps[0][0:1, 128:256],
                             AF.Identity, deps=[t3] + ([] if ev_r2 else use4["sprow"]),
                             tag=f"q2re{th_}_{tb}"))
        ec1 = ts2(sprow[0:1, 0:512], sprow[0:1, 0:512], scl[0:1, S_K2:S_K2 + 1], 1e-5,
                  ALU.mult, ALU.max, deps=ev_r2 + [ek2a], tag=f"q2cl{th_}")
        ec2 = recip(sprow[0:1, 512:1024], sprow[0:1, 0:512], deps=[ec1],
                    tag=f"q2rc{th_}")
        b1 = mm(f2_ps[0][:, 0:512], ones[0:1, :], sprow[0:1, 0:512], True, True,
                deps=[ec1, ev_r2[-1]], tag=f"q2bm{th_}")
        e_rb = act(rB2[:], f2_ps[0][:, 0:512], AF.Identity, deps=[b1, use4["rB2"]],
                   tag=f"q2be{th_}")
        b2 = mm(f2_ps[0][:, 0:512], ones[0:1, :], sprow[0:1, 512:1024], True, True,
                deps=[ec2, e_rb], tag=f"q2bm2{th_}")
        e_sb = act(sB2[:], f2_ps[0][:, 0:512], AF.Identity, deps=[b2, eM],
                   tag=f"q2be2{th_}")
        er1 = ts1(rB2[:], rB2[:], 1.0 / 127.0, ALU.mult, deps=[e_rb],
                  tag=f"q2rs{th_}")
        er2 = add("dve", lambda h: h.tensor_single_scalar(
            sB2[:], sB2[:], scl[:, S_K2_127:S_K2_127 + 1], ALU.mult),
            deps=[e_sb, ek2b], tag=f"q2ss{th_}")
        use4["sprow"] = [b1, b2, ec2]
        use4["Mq2"] = t1
        use4["sB2"] = er2
        return {"rB": er1, "sB": er2, "rows_last": ev_r2[-1],
                "ps_rd": [e_rb, e_sb, ev_r2[-1]]}

    def emit_quant_pair(th_, p, ev_mix, sc_):
        zp3 = zS[th_][:, 1024 * p:1024 * (p + 1)].rearrange(
            "p (two f) -> p two f", f=512)
        tp3 = tscr2[:].rearrange("p (two f) -> p two f", f=512)
        sbb = sB2[:].unsqueeze(1).broadcast_to([128, 2, 512])
        rbb = rB2[:].unsqueeze(1).broadcast_to([128, 2, 512])
        et = add("pool", lambda h: h.tensor_tensor(tp3, zp3, sbb, ALU.mult),
                 deps=[ev_mix, sc_["sB"], use4["tscr"]], tag=f"q2t{th_}_{p}")
        ez1 = act(z1q[:], tscr2[:], AF.Abs, deps=[et, use4["z1q"]],
                  tag=f"q2z1_{th_}_{p}")
        for ci, cf in enumerate(C_FOLDS):
            ez1 = act(z1q[:], z1q[:], AF.Abs, bias=cc(cf), deps=[ez1],
                      tag=f"q2f1_{th_}_{p}_{ci}")
        if p % 2 == 0:
            ez2 = act(z2q[:], tscr2[:], AF.Abs, bias=cc(C_NEGQ),
                      deps=[et, use4["z2q"]], tag=f"q2da_{th_}_{p}")
            for ci, cf in enumerate(C_FOLDS):
                ez2 = act(z2q[:], z2q[:], AF.Abs, bias=cc(cf), deps=[ez2],
                          tag=f"q2dg{th_}_{p}_{ci}")
        else:
            d0 = ts1(z2q[:], tscr2[:], -0.25, ALU.add,
                     deps=[et, use4["z2q"]], tag=f"q2d0_{th_}_{p}")
            ez2 = stt(z2q[:], z2q[:], -1.0, z2q[:], ALU.mult, ALU.max,
                      deps=[d0], tag=f"q2da_{th_}_{p}")
            for ci in range(8):
                vv = float(-(2.0 ** (6 - ci)) if ci < 7 else -0.5)
                f0 = ts1(z2q[:], z2q[:], vv, ALU.add, deps=[ez2],
                         tag=f"q2df{th_}_{p}_{ci}")
                ez2 = stt(z2q[:], z2q[:], -1.0, z2q[:], ALU.mult, ALU.max,
                          deps=[f0], tag=f"q2dg{th_}_{p}_{ci}")
        eg = ts1(z2q[:], z2q[:], 0.25, ALU.is_ge, deps=[ez2], tag=f"q2g{th_}_{p}")
        esg = ts2(z2q[:], z2q[:], 2.0, -1.0, ALU.mult, ALU.add, deps=[eg],
                  tag=f"q2sg{th_}_{p}")
        ed1 = ts2(z1q[:], z1q[:], -1.0, 0.5, ALU.mult, ALU.add, deps=[ez1],
                  tag=f"q2d1{th_}_{p}")
        ee = ptt(z1q[:], z1q[:], z2q[:], ALU.mult, deps=[ed1, esg],
                 tag=f"q2e{th_}_{p}")
        eq = ptt(tscr2[:], tscr2[:], z1q[:], ALU.subtract, deps=[ee],
                 tag=f"q2q{th_}_{p}")
        euq = add("pool", lambda h: h.tensor_tensor(zp3, tp3, rbb, ALU.mult),
                  deps=[eq, sc_["rB"]], tag=f"q2u{th_}_{p}")
        use4["tscr"] = euq
        use4["z1q"] = eq
        use4["z2q"] = ee
        ev_u[(th_, 2 * p)] = euq
        ev_u[(th_, 2 * p + 1)] = euq

    dstate = {"prev_mm": aru.get("last_pe"), "edb": {}, "edt": {}, "etr": {},
              "est": {}}

    def emit_down_ob(th_, ob):
        elA = dma("sync", arA[:].rearrange("p (fc o) -> p fc o", o=128),
                  wdT3[:, 0:32, 128 * ob:128 * (ob + 1)], deps=[aru["A"]],
                  tag=f"dldA{th_}_{ob}")
        elB = dma("pool", arB[:].rearrange("p (fc o) -> p fc o", o=128),
                  wdT3[:, 32:64, 128 * ob:128 * (ob + 1)], deps=[aru["B"]],
                  tag=f"dldB{th_}_{ob}")
        tern_evs = []
        for half, (ar_, el_) in enumerate(((arA, elA), (arB, elB))):
            sl = slice(4096 * half, 4096 * (half + 1))
            t1 = add("dve", lambda h, sl=sl, ar_=ar_: h.tensor_single_scalar(
                ternd[:, sl], ar_[:], scl[:, S_PHD:S_PHD + 1], ALU.is_ge),
                deps=[el_, dstate.get(("trd", th_, ob - 1))] + ev_sd,
                tag=f"dt1_{th_}_{ob}_{half}")
            t2 = add("dve", lambda h, ar_=ar_: h.tensor_single_scalar(
                ar_[:], ar_[:], scl[:, S_NHD:S_NHD + 1], ALU.is_le),
                deps=[t1], tag=f"dt2_{th_}_{ob}_{half}")
            t3 = tt(ternd[:, sl], ternd[:, sl], ar_[:], ALU.subtract, deps=[t2],
                    tag=f"dt3_{th_}_{ob}_{half}")
            tern_evs.append(t3)
            if half == 0:
                aru["A"] = t3
            else:
                aru["B"] = t3
        pb = d_ps[ob % 2]
        lmm = None
        for fc in range(64):
            dep = [tern_evs[fc // 32], ev_u[(th_, fc)]]
            if fc == 0:
                dep += [dstate["edb"].get((th_, ob - 2)) or dstate["edb"].get((th_ - 1, 14 + ob)),
                        dstate["prev_mm"]]
            lmm = mm(pb[:], ternd[:, 128 * fc:128 * (fc + 1)], zsl(th_, fc),
                     fc == 0, fc == 63, deps=dep, tag=f"dmm{th_}_{ob}_{fc}")
        dstate["prev_mm"] = lmm
        edb = act(dob[:], pb[:], AF.Identity,
                  deps=[lmm] + dstate["etr"].get(("prev",), []),
                  tag=f"dev{th_}_{ob}")
        dstate["edb"][(th_, ob)] = edb
        etr = []
        for j2 in range(4):
            etr.append(mm(dT_ps[:, 128 * j2:128 * (j2 + 1)],
                          dob[:, 128 * j2:128 * (j2 + 1)], i128b[:], True, True,
                          deps=[edb, ev_ib, dstate["edt"].get(("prev",))],
                          tag=f"dtr{th_}_{ob}_{j2}"))
        dstate["etr"][("prev",)] = etr
        edt = act(dtb[:], dT_ps[:], AF.Identity,
                  deps=etr + dstate["est"].get(("prev",), []),
                  tag=f"dte{th_}_{ob}")
        dstate["edt"][("prev",)] = edt
        ests = []
        for j2 in range(4):
            ests.append(dma("sync",
                            g["dn2_hbm"][512 * th_ + 128 * j2:512 * th_ + 128 * (j2 + 1),
                                         128 * ob:128 * (ob + 1)],
                            dtb[:, 128 * j2:128 * (j2 + 1)], deps=[edt],
                            tag=f"dst{th_}_{ob}_{j2}"))
        dstate["est"][("prev",)] = ests
        for j2 in range(4):
            ev_dn2[(th_, j2, ob)] = ests[j2]
        dstate[("trd", th_, ob)] = lmm

    def emit_out_step(th_, tql, qd):
        tqg = 4 * th_ + tql
        dl = dma("sync", dnt[:],
                 g["dn2_hbm"][512 * th_ + 128 * tql:512 * th_ + 128 * (tql + 1),
                              512 * qd:512 * (qd + 1)],
                 deps=[ev_dn2[(th_, tql, ob)] for ob in range(4 * qd, 4 * (qd + 1))]
                 + [use4["dnt"]], tag=f"onl{th_}_{tql}_{qd}")
        xl = dma("sync", xtd[:],
                 g["x1_hbm"][128 * tqg:128 * (tqg + 1), 512 * qd:512 * (qd + 1)],
                 deps=[ev_x1_st[tqg], use4["xtd"]], tag=f"oxl{th_}_{tql}_{qd}")
        e1 = tt(d2_ps[:], dnt[:], gbT2[:, 512 * qd:512 * (qd + 1)], ALU.mult,
                deps=[dl, ev_gb2, use4["d2"]], tag=f"ogb{th_}_{tql}_{qd}")
        e2 = tt(xtd[:], xtd[:], d2_ps[:], ALU.add, deps=[xl, e1],
                tag=f"ox2{th_}_{tql}_{qd}")
        est = dma("sync", g["out"][128 * tqg:128 * (tqg + 1), 512 * qd:512 * (qd + 1)],
                  xtd[:], deps=[e2], tag=f"ost{th_}_{tql}_{qd}")
        use4["dnt"] = e1
        use4["xtd"] = est
        use4["d2"] = e2

    evhA = phase_load_h128(0)
    mxA = phase_mix(0, evhA)
    scA = phase_amax_scales(0, mxA)
    for p in range(32):
        emit_quant_pair(0, p, mxA, scA)
    evhB = phase_load_h128(1, extra=scA["ps_rd"] + [evhA[62], evhA[63]])
    mxB = phase_mix(1, evhB)
    emit_down_ob(0, 0)
    emit_down_ob(0, 1)
    scB = phase_amax_scales(1, mxB)
    qa = 0
    for ob in range(2, 16):
        emit_down_ob(0, ob)
        for _ in range(3):
            if qa < 32:
                emit_quant_pair(1, qa, mxB, scB)
                qa += 1
    while qa < 32:
        emit_quant_pair(1, qa, mxB, scB)
        qa += 1
    outsteps = [(tql, qd) for tql in range(4) for qd in range(4)]
    oi = 0
    for ob in range(16):
        emit_down_ob(1, ob)
        if oi < len(outsteps):
            emit_out_step(0, *outsteps[oi])
            oi += 1
    while oi < len(outsteps):
        emit_out_step(0, *outsteps[oi])
        oi += 1
    for tql in range(4):
        for qd in range(4):
            emit_out_step(1, tql, qd)
    c5.close()
    c4.close()
    czh.close()


_cache = {}


def kernel(L0_attn_x, L0_attn_wq, L0_attn_wk, L0_attn_wv, L0_attn_wo,
           L0_attn_gq, L0_attn_gk, L0_ffn_w_gate, L0_ffn_w_up, L0_ffn_w_down,
           L0_c, L0_wmod):
    x = np.ascontiguousarray(np.asarray(L0_attn_x, dtype=np.float32))
    wq = np.ascontiguousarray(np.asarray(L0_attn_wq, np.float32))
    wk = np.ascontiguousarray(np.asarray(L0_attn_wk, np.float32))
    wv = np.ascontiguousarray(np.asarray(L0_attn_wv, np.float32))
    wo = np.ascontiguousarray(np.asarray(L0_attn_wo, np.float32))
    gq = np.ascontiguousarray(np.asarray(L0_attn_gq, np.float32).reshape(1, D))
    gk = np.ascontiguousarray(np.asarray(L0_attn_gk, np.float32).reshape(1, 512))
    wgT = np.ascontiguousarray(np.asarray(L0_ffn_w_gate, np.float32).T)
    wuT = np.ascontiguousarray(np.asarray(L0_ffn_w_up, np.float32).T)
    wdT = np.ascontiguousarray(np.asarray(L0_ffn_w_down, np.float32).T)
    cvec = np.ascontiguousarray(np.asarray(L0_c, np.float32))
    wmod = np.ascontiguousarray(np.asarray(L0_wmod, np.float32))

    cst, ones_h, i128f_h, h128_h, diag_h, ropeC, ropeS = _host_consts()
    if "nc" not in _cache:
        _cache["nc"] = build_program()
    nc = _cache["nc"]

    in_maps = []
    metas = []
    for core in range(8):
        b = core // 2
        par = core % 2
        kv, qtl, dead = _parity_layout(par)
        xb = x[b]
        x_kv = np.concatenate([xb[128 * s:128 * (s + 1)] for s in kv], 0)
        x_q = np.concatenate([xb[128 * s:128 * (s + 1)] for s in qtl], 0)
        rC_kv = np.concatenate([ropeC[128 * s:128 * (s + 1)] for s in kv], 0)
        rS_kv = np.concatenate([ropeS[128 * s:128 * (s + 1)] for s in kv], 0)
        rC_q = np.concatenate([ropeC[128 * s:128 * (s + 1)] for s in qtl], 0)
        rS_q = np.concatenate([ropeS[128 * s:128 * (s + 1)] for s in qtl], 0)
        im = dict(
            x_kv=np.ascontiguousarray(x_kv), x_q=np.ascontiguousarray(x_q),
            ropeC_kv=np.ascontiguousarray(rC_kv), ropeS_kv=np.ascontiguousarray(rS_kv),
            ropeC_q=np.ascontiguousarray(rC_q), ropeS_q=np.ascontiguousarray(rS_q),
            deadrow=np.ascontiguousarray(dead.reshape(1, 32)),
            wq=wq, wk=wk, wv=wv, wo=wo, wgT=wgT, wuT=wuT, wdT=wdT, wmod=wmod,
            c_vec=np.ascontiguousarray(cvec[b:b + 1]), gq_row=gq, gk_row=gk,
            cstab=cst, ones=ones_h, i128f=i128f_h, h128=h128_h, diagmask=diag_h,
        )
        in_maps.append(im)
        metas.append((b, qtl))

    res = run_bass_kernel_spmd(nc, in_maps, core_ids=list(range(8)))
    out = np.zeros((B, T, D), np.float32)
    for core in range(8):
        b, qtl = metas[core]
        o = res.results[core]["out"]
        for i, s in enumerate(qtl):
            out[b, 128 * s:128 * (s + 1), :] = o[128 * i:128 * (i + 1), :]
    return out


# version 12 (final)
# speedup vs baseline: 1.0131x; 1.0131x over previous
import numpy as np
import ml_dtypes
import concourse.bass as bass
import concourse.mybir as mybir
from concourse.bass_utils import run_bass_kernel_spmd
from contextlib import ExitStack

F32 = mybir.dt.float32
BF16 = mybir.dt.bfloat16
ALU = mybir.AluOpType
AF = mybir.ActivationFunctionType
AX = mybir.AxisListType

B, T, D = 4, 2048, 2048
DH, NG, NH = 128, 4, 16
I = 8192
TQ = 1024
NKV = 16
MULT = float(DH) ** (-0.25)
SQ2_127 = float(np.float32(np.sqrt(2.0)) * np.float32(127.0))

ENGINES = ("pe", "act", "dve", "pool", "sync")

C_FOLDS = [0, 1, 2, 3, 4, 5, 6, 7]
C_NEGQ, C_ONE, C_ZERO, C_MULT, C_NEG1 = 8, 9, 10, 11, 12

S_HG, S_NHG, S_PHG = 0, 1, 2
S_HU, S_NHU, S_PHU = 3, 4, 5
S_HD, S_NHD, S_PHD = 6, 7, 8
S_K2, S_K2_127, S_LAMD = 9, 10, 11


class _Op:
    __slots__ = ("eng", "fn", "deps", "gid", "need_inc", "sem_val", "is_dma",
                 "tag", "lane")

    def __init__(self, eng, fn, deps, gid, is_dma, tag):
        self.eng, self.fn, self.deps, self.gid = eng, fn, deps, gid
        self.need_inc, self.sem_val, self.is_dma, self.tag = False, None, is_dma, tag
        self.lane = None


NLANES = 8


class Sched:
    def __init__(self):
        self.ops = {e: [] for e in ENGINES}
        self.all = []

    def add(self, eng, fn, deps=(), dma=False, tag=""):
        deps = [d for d in deps if d is not None]
        for d in deps:
            assert isinstance(d, _Op), f"bad dep for {tag}"
        op = _Op(eng, fn, deps, len(self.all), dma, tag)
        self.ops[eng].append(op)
        self.all.append(op)
        return op

    def emit(self, nc):
        for op in self.all:
            for d in op.deps:
                if (d.eng != op.eng or d.is_dma or op.is_dma
                        or d.eng in ("act", "pool", "dve")):
                    d.need_inc = True
        counters = {e: 0 for e in ENGINES}
        dma_cnt = {}
        dma_keys = []
        for e in ENGINES:
            for op in self.ops[e]:
                if op.is_dma:
                    op.need_inc = True
                    nla = dma_cnt.setdefault(e, [0] * NLANES)
                    tot = sum(nla)
                    op.lane = tot % NLANES
                    nla[op.lane] += 1
                    op.sem_val = 16 * nla[op.lane]
                    if (e, op.lane) not in dma_keys:
                        dma_keys.append((e, op.lane))
                elif op.need_inc:
                    counters[e] += 1
                    op.sem_val = counters[e]
        finals = {}
        for e in ENGINES:
            comp = [op for op in self.ops[e] if not op.is_dma]
            if comp:
                op = comp[-1]
                if not op.need_inc:
                    op.need_inc = True
                    counters[e] += 1
                    op.sem_val = counters[e]
                finals[e] = op.sem_val
        dma_finals = {}
        for e, nla in dma_cnt.items():
            for ln in range(NLANES):
                if nla[ln]:
                    dma_finals[(e, ln)] = 16 * nla[ln]

        def key_of(op):
            return (op.eng, op.lane) if op.is_dma else op.eng

        with ExitStack() as ctx:
            EE = ctx.enter_context
            sems = {e: EE(nc.semaphore(f"s_{e}")) for e in ENGINES}
            for k in dma_keys:
                sems[k] = EE(nc.semaphore(f"d_{k[0]}_{k[1]}"))
            block = EE(nc.Block())
            eng_attr = {"pe": "tensor", "act": "scalar", "dve": "vector",
                        "pool": "gpsimd", "sync": "sync"}

            def mk(e):
                ops = self.ops[e]

                def body(h):
                    waited = {}
                    def wait(k, v):
                        if waited.get(k, 0) < v:
                            h.wait_ge(sems[k], v)
                            waited[k] = v
                    for op in ops:
                        need = {}
                        for d in op.deps:
                            k = key_of(d)
                            if k == e and not d.is_dma and e == "pe":
                                continue
                            need[k] = max(need.get(k, 0), d.sem_val)
                        if op.is_dma and op.sem_val > 16:
                            need[key_of(op)] = max(need.get(key_of(op), 0),
                                                   op.sem_val - 16)
                        for k in sorted(need, key=str):
                            wait(k, need[k])
                        try:
                            inst = op.fn(h)
                        except Exception as exc:
                            raise RuntimeError(
                                f"emit failed at op {op.tag}: {exc}") from exc
                        if op.need_inc:
                            assert inst is not None, f"{op.tag}: inc-op returned None"
                            inst.then_inc(sems[key_of(op)], 16 if op.is_dma else 1)
                    if e == "pe":
                        for oe in ENGINES:
                            if oe != e and oe in finals:
                                wait(oe, finals[oe])
                        for k, v in dma_finals.items():
                            wait(k, v)
                return body

            for e in ENGINES:
                getattr(block, eng_attr[e])(mk(e))
        return nc


def _hadamard(n):
    h = np.array([[1.0]])
    while h.shape[0] < n:
        h = np.block([[h, h], [h, -h]])
    return h


def _host_consts():
    cst = np.zeros((128, 32), np.float32)
    vals = dict(zip(C_FOLDS, [-64.0, -32.0, -16.0, -8.0, -4.0, -2.0, -1.0, -0.5]))
    vals.update({C_NEGQ: -0.25, C_ONE: 1.0, C_ZERO: 0.0, C_MULT: MULT,
                 C_NEG1: -1.0})
    for k, v in vals.items():
        cst[:, k] = np.float32(v)
    ones = np.ones((128, 128), np.float32)
    i128f = np.eye(128, dtype=np.float32)
    h128 = np.asarray((_hadamard(128) * 2.0 ** -6).astype(ml_dtypes.bfloat16))
    kk = np.arange(128)
    diagmask = np.where(kk[None, :] < kk[:, None], -1e30, 0.0).astype(np.float32)
    pos = np.arange(T, dtype=np.float64)[:, None]
    dd = np.arange(64, dtype=np.float64)[None, :]
    rad = pos * np.power(10000.0, -dd / 64.0)
    ropeC = np.concatenate([np.cos(rad), np.cos(rad)], 1).astype(np.float32)
    ropeS = np.concatenate([-np.sin(rad), np.sin(rad)], 1).astype(np.float32)
    return cst, ones, i128f, h128, diagmask, ropeC, ropeS


def _parity_layout(par):
    if par == 0:
        kv = list(range(16))
        qt = [0, 1, 2, 3, 12, 13, 14, 15]
        dead = np.zeros((2, 16), np.float32)
        dead[0, 4:8] = -1e30
    else:
        kv = [4, 5, 6, 7, 8, 9, 10, 11, 0, 1, 2, 3, 8, 9, 10, 11]
        qt = [4, 5, 6, 7, 8, 9, 10, 11]
        dead = np.zeros((2, 16), np.float32)
        dead[1, 8:12] = -1e30
    return kv, qt, dead


def _slots(qt):
    out = []
    fs = [8, 9, 10, 11] if qt == 0 else list(range(12))
    cs = [0, 1, 2, 3] if qt == 0 else [12, 13, 14, 15]
    for j, s in enumerate(cs):
        out.append(("C", s, 128 * j, 512 - 128 * j))
    for s in fs:
        out.append(("F", s, 0, 512))
    return out


def build_program(debug=()):
    nc = bass.Bass("TRN2", target_bir_lowering=False)
    g = {}

    def di(name, shape, dt=F32):
        g[name] = nc.dram_tensor(name, list(shape), dt, kind="ExternalInput").ap()

    di("x_kv", (NKV * 128, D)); di("x_q", (TQ, D))
    di("ropeC_kv", (NKV * 128, 128)); di("ropeS_kv", (NKV * 128, 128))
    di("ropeC_q", (TQ, 128)); di("ropeS_q", (TQ, 128))
    di("deadrow", (1, 32))
    di("wq", (D, D)); di("wk", (D, 512)); di("wv", (D, 512)); di("wo", (D, D))
    di("wgT", (D, I)); di("wuT", (D, I)); di("wdT", (I, D))
    di("wmod", (D, 6 * D))
    di("c_vec", (1, D)); di("gq_row", (1, D)); di("gk_row", (1, 512))
    di("cstab", (128, 32)); di("ones", (128, 128)); di("i128f", (128, 128))
    di("h128", (128, 128), BF16); di("diagmask", (128, 128))
    g["out"] = nc.dram_tensor("out", [TQ, D], F32, kind="ExternalOutput").ap()
    for nm, shape, dt in (("x1_hbm", (TQ, D), F32), ("part_hbm", (TQ, D), F32),
                          ("z_hbm", (I, TQ), BF16), ("dn2_hbm", (TQ, D), BF16),
                          ("sbcb_hbm", (256, D), BF16),
                          ("gb_hbm", (128, D), BF16)):
        g[nm] = nc.dram_tensor(nm, list(shape), dt, kind="Internal").ap()
    dbg = {}
    for name, shape, dt in debug:
        g["dbg_" + name] = nc.dram_tensor("dbg_" + name, list(shape), dt,
                                          kind="ExternalOutput").ap()
        dbg[name] = g["dbg_" + name]

    sch = Sched()
    with ExitStack() as octx:
        _prog(nc, g, sch, octx.enter_context, dbg)
        sch.emit(nc)
    return nc


def _prog(nc, g, sch, EO, dbg):
    add = sch.add

    def sb_(E, name, p, f, dt=BF16):
        return E(nc.sbuf_tensor(name, [p, f], dt))

    def ps_(E, name, p, f):
        return E(nc.psum_tensor(name, [p, f], F32))

    def dma(eng, out, in_, deps=(), tag=""):
        return add(eng, lambda h, out=out, in_=in_: h.dma_start(out=out, in_=in_),
                   dma=True, deps=deps, tag=tag)

    def dmaT(out, in_, deps=(), tag=""):
        return add("sync", lambda h, out=out, in_=in_: h.dma_start_transpose(out, in_),
                   dma=True, deps=deps, tag=tag)

    def mm(out, lhsT, rhs, start, stop, deps=(), tag=""):
        return add("pe", lambda h, o=out, l=lhsT, r=rhs, s1=start, s2=stop:
                   h.matmul(o, l, r, start=s1, stop=s2), deps=deps, tag=tag)

    def act(out, in_, func, bias=None, scale=None, accum=None, deps=(), tag=""):
        def f(h, out=out, in_=in_, func=func, bias=bias, scale=scale, accum=accum):
            kw = {}
            if bias is not None:
                kw["bias"] = bias
            if scale is not None:
                kw["scale"] = scale
            if accum is not None:
                kw["accum_out"] = accum
            return h.activation(out, in_, func, **kw)
        return add("act", f, deps=deps, tag=tag)

    def tt(out, a, b, op, deps=(), tag=""):
        return add("dve", lambda h, o=out, a=a, b=b, op=op:
                   h.tensor_tensor(o, a, b, op), deps=deps, tag=tag)

    def tcp(out, in_, deps=(), tag=""):
        return add("dve", lambda h, o=out, i=in_: h.tensor_copy(o, i), deps=deps, tag=tag)

    def ts1(out, in_, s1, op0, deps=(), tag=""):
        return add("dve", lambda h, o=out, i=in_, s1=s1, op0=op0:
                   h.tensor_single_scalar(o, i, s1, op0), deps=deps, tag=tag)

    def ts2(out, in_, s1, s2, op0, op1, deps=(), tag=""):
        return add("dve", lambda h, o=out, i=in_, s1=s1, s2=s2, op0=op0, op1=op1:
                   h.tensor_scalar(o, i, s1, s2, op0, op1), deps=deps, tag=tag)

    def stt(out, in0, s, in1, op0, op1, deps=(), tag=""):
        return add("dve", lambda h, o=out, i0=in0, s=s, i1=in1, op0=op0, op1=op1:
                   h.scalar_tensor_tensor(o, i0, s, i1, op0, op1), deps=deps, tag=tag)

    def recip(out, in_, deps=(), tag=""):
        return add("dve", lambda h, o=out, i=in_: h.reciprocal(o, i), deps=deps, tag=tag)

    def redmax(out, in_, deps=(), tag=""):
        return add("dve", lambda h, o=out, i=in_: h.reduce_max(o, i, axis=AX.X),
                   deps=deps, tag=tag)

    cst = sb_(EO, "cst", 128, 32, F32)
    ones = sb_(EO, "ones_t", 128, 128, F32)
    onesb = sb_(EO, "onesb_t", 128, 4, BF16)
    i128f = sb_(EO, "i128f_t", 128, 128, F32)
    h128 = sb_(EO, "h128_t", 128, 128, BF16)
    dmask = sb_(EO, "dmask_t", 128, 128, F32)
    deadr = sb_(EO, "dead_t", 1, 32, F32)
    i128b = sb_(EO, "i128b", 128, 128, BF16)
    arA = sb_(EO, "arA", 128, 4096, F32)
    arB = sb_(EO, "arB", 128, 4096, F32)
    scl = sb_(EO, "scl", 128, 16, F32)
    wacc = sb_(EO, "wacc", 128, 8, F32)
    ppb = [None, None]
    rowsc = sb_(EO, "rowsc", 1, 64, F32)
    sprow = sb_(EO, "sprow", 1, 1024, F32)
    arAb = arA[:].bitcast(BF16)
    arBb = arB[:].bitcast(BF16)
    aru = {"A": None, "B": None}

    def cc(col):
        return cst[:, col:col + 1]

    evC = [dma("sync", dst[:], g[nm], tag=f"ld_{nm}")
           for nm, dst in (("cstab", cst), ("ones", ones), ("i128f", i128f),
                           ("h128", h128), ("diagmask", dmask), ("deadrow", deadr))]
    ev_onesb = add("dve", lambda h: h.memset(onesb[:], 1.0), tag="onesb")
    ev_ib = ts1(i128b[:], i128f[:], 1.0, ALU.mult, deps=evC, tag="i128b")

    pp_list = []
    for nm, rows in (("wgT", D), ("wuT", D), ("wdT", I)):
        src3 = g[nm].rearrange("(n p) c -> n p c", p=128)
        width = g[nm].shape[1]
        for i in range(rows // 128):
            for hcol in range(width // 1024):
                pp_list.append((nm, src3[i, :, 1024 * hcol:1024 * (hcol + 1)]))
    WCOL = {"wgT": 0, "wuT": 1, "wdT": 2}
    ev_pp_zero = add("dve", lambda h: h.memset(wacc[:, 0:5], 0.0), tag="pp_zero")
    pp = {"i": 0, "use": [None, None], "acc": {k: ev_pp_zero for k in WCOL},
          "last": {}, "scr_rd": [None, None]}

    def pp_step(n=1):
        for _ in range(n):
            i = pp["i"]
            if i >= len(pp_list):
                return
            pp["i"] = i + 1
            nm, src = pp_list[i]
            lane = i % 2
            buf = ppb[lane]
            scr = 3 + lane
            ld = dma("sync", buf[:], src, deps=[pp["use"][lane]], tag=f"pp_ld{i}")
            a = act(buf[:], buf[:], AF.Abs, accum=wacc[:, scr:scr + 1],
                    deps=[ld, pp["scr_rd"][lane]], tag=f"pp_abs{i}")
            pp["use"][lane] = a
            wc = WCOL[nm]
            acc = tt(wacc[:, wc:wc + 1], wacc[:, wc:wc + 1], wacc[:, scr:scr + 1],
                     ALU.add, deps=[a, pp["acc"][nm]], tag=f"pp_acc{i}")
            pp["acc"][nm] = acc
            pp["scr_rd"][lane] = acc
            pp["last"][nm] = acc

    def rsqrt_chain(pfx, v_ap, w, scr, deps):
        s0 = scr[:, 0 * w:1 * w]; t1 = scr[:, 1 * w:2 * w]
        t2 = scr[:, 2 * w:3 * w]; s1 = scr[:, 3 * w:4 * w]
        e0 = act(s0, v_ap, AF.Sqrt, deps=deps, tag=pfx + "sq")
        e1 = recip(t1, s0, deps=[e0], tag=pfx + "r0")
        e2 = tt(t2, v_ap, t1, ALU.mult, deps=[e1], tag=pfx + "vm")
        e3 = tt(s1, s0, t2, ALU.add, deps=[e2], tag=pfx + "ad")
        e4 = ts1(s1, s1, 0.5, ALU.mult, deps=[e3], tag=pfx + "hf")
        return recip(v_ap, s1, deps=[e4], tag=pfx + "rf")

    c1 = ExitStack()
    E1 = c1.enter_context
    saT = sb_(E1, "saT", 128, D, BF16)
    caT = sb_(E1, "caT", 128, D, BF16)
    gaT = sb_(E1, "gaT", 128, D, BF16)
    kfm = [sb_(E1, f"kfm{i}", 128, 2048, BF16) for i in range(4)]
    vtm = sb_(E1, "vtm", 128, 8192, BF16)
    qfm = sb_(E1, "qfm", 128, 16384, BF16)
    stats = sb_(E1, "stats", 128, 32, F32)
    msq = sb_(E1, "msq", 128, 16, F32)
    knrm = sb_(E1, "knrm", 128, 64, F32)
    cmax = sb_(E1, "cmax", 128, 32, F32)
    mgb = sb_(E1, "mgb", 128, 4, F32)
    small_ps = ps_(E1, "small_ps", 128, 512)
    pe_ps = [ps_(E1, f"pe_ps{j}", 128, 512) for j in range(4)]
    pe_ps2 = [ps_(E1, f"pe_ps2_{j}", 128, 512) for j in range(3)]
    ppb[0] = sb_(E1, "ppb0", 128, 1024, F32)
    ppb[1] = sb_(E1, "ppb1", 128, 1024, F32)
    gqmB = sb_(E1, "gqmB", 128, D, BF16)
    gkmB = sb_(E1, "gkmB", 128, 512, BF16)
    dmp = sb_(E1, "dmp", 128, 2048, BF16)

    cB = ExitStack()
    EB = cB.enter_context
    crow_s = sb_(EB, "crow", 1, D, F32)
    silc = sb_(EB, "silc", 1, D, F32)
    scol = sb_(EB, "scol", 128, 16, F32)
    repc = sb_(EB, "repc", 128, D, BF16)
    garw = sb_(EB, "garw", 128, D, F32)
    gbrw = sb_(EB, "gbrw", 128, D, F32)
    sbT0 = sb_(EB, "sbT0", 128, D, BF16)
    cbT0 = sb_(EB, "cbT0", 128, D, BF16)
    gbT0 = sb_(EB, "gbT0", 128, D, BF16)
    grow = sb_(EB, "grow", 1, D, F32)
    gkrow = sb_(EB, "gkrow", 1, 512, F32)
    bscr = sb_(EB, "bscr", 128, 4096, F32)

    ev_c = dma("sync", crow_s[:], g["c_vec"], tag="ld_c")
    ev_silc = act(silc[:], crow_s[:], AF.Silu, deps=[ev_c], tag="silu_c")
    ev_scol_mm = []
    for kc in range(16):
        ev_scol_mm.append(mm(small_ps[:, kc:kc + 1], silc[0:1, 128 * kc:128 * (kc + 1)],
                             ones[0:1, 0:1], True, True,
                             deps=[ev_silc] + evC, tag=f"scolmm{kc}"))
    ev_scol = act(scol[:], small_ps[:, 0:16], AF.Identity, deps=ev_scol_mm, tag="scol_ev")
    ev_repc = []
    for kc in range(16):
        ev_repc.append(act(repc[:, 128 * kc:128 * (kc + 1)], ones[:], AF.Identity,
                           scale=scol[:, kc:kc + 1], deps=[ev_scol], tag=f"repc{kc}"))

    ev_gq = dma("sync", grow[:], g["gq_row"], tag="ld_gq")
    ev_gk = dma("sync", gkrow[:], g["gk_row"], tag="ld_gk")
    ev_gqm0 = ts1(grow[:], grow[:], MULT, ALU.mult, deps=[ev_gq], tag="gqmul")
    ev_gkm0 = ts1(gkrow[:], gkrow[:], MULT, ALU.mult, deps=[ev_gk], tag="gkmul")
    prev = ev_scol
    for j in range(4):
        e1 = mm(small_ps[:, 0:512], ones[0:1, :], grow[:, 512 * j:512 * (j + 1)],
                True, True, deps=[ev_gqm0, prev] + evC, tag=f"gqm{j}")
        prev = act(gqmB[:, 512 * j:512 * (j + 1)], small_ps[:, 0:512], AF.Identity,
                   deps=[e1], tag=f"gqme{j}")
    e1 = mm(small_ps[:, 0:512], ones[0:1, :], gkrow[:], True, True,
            deps=[ev_gkm0, prev], tag="gkm")
    ev_gkm = act(gkmB[:], small_ps[:, 0:512], AF.Identity, deps=[e1], tag="gkme")

    wmod3 = g["wmod"].rearrange("(kc p) c -> p kc c", p=128)
    raw_names = ["sa", "ca", "ga", "sb", "cb", "gb"]
    prev_mm_h = [None, None]
    ev_mod_ev = {}
    for grp in range(24):
        half = grp % 2
        dstb = arAb if half == 0 else arBb
        ld = add("pool", lambda h, grp=grp, dstb=dstb: h.dma_start(
            out=dstb[0:128, 0:8192].rearrange("p (kc c) -> p kc c", c=512),
            in_=wmod3[:, :, 512 * grp:512 * (grp + 1)]),
            dma=True, deps=[prev_mm_h[half]] + evC, tag=f"wmld{grp}")
        last = None
        for kc in range(16):
            dep = [ld]
            if kc == 0:
                dep += ev_repc
                if grp >= 2:
                    dep += [ev_mod_ev[grp - 2]]
            last = mm(pe_ps[half][:], repc[:, 128 * kc:128 * (kc + 1)],
                      dstb[0:128, 512 * kc:512 * (kc + 1)], kc == 0, kc == 15,
                      deps=dep, tag=f"wmmm{grp}_{kc}")
        prev_mm_h[half] = last
        nm = raw_names[grp // 4]
        off = (grp % 4) * 512
        if nm in ("ca", "cb"):
            tgt = caT if nm == "ca" else cbT0
            ev = act(tgt[:, off:off + 512], pe_ps[half][:], AF.Identity,
                     bias=cc(C_ONE), deps=[last], tag=f"wmev{grp}")
        elif nm in ("sa", "sb"):
            tgt = saT if nm == "sa" else sbT0
            ev = act(tgt[:, off:off + 512], pe_ps[half][:], AF.Identity,
                     deps=[last], tag=f"wmev{grp}")
        else:
            raw = garw if nm == "ga" else gbrw
            ev = act(raw[:, off:off + 512], pe_ps[half][:], AF.Identity,
                     deps=[last], tag=f"wmev{grp}")
        ev_mod_ev[grp] = ev
    ev_sa = [ev_mod_ev[j] for j in range(0, 4)]
    ev_ca = [ev_mod_ev[j] for j in range(4, 8)]
    ev_ga = [ev_mod_ev[j] for j in range(8, 12)]
    ev_sbcb = [ev_mod_ev[j] for j in range(12, 20)]
    ev_gb = [ev_mod_ev[j] for j in range(20, 24)]
    aru["A"] = prev_mm_h[0]
    aru["B"] = prev_mm_h[1]

    def gate_tile(pfx, raw, out_t, deps):
        s1 = ts1(bscr[:, 0:2048], raw[:], 0.0, ALU.is_ge, deps=deps, tag=pfx + "ge")
        s2 = ts2(bscr[:, 0:2048], bscr[:, 0:2048], 2.0, -1.0, ALU.mult, ALU.add,
                 deps=[s1], tag=pfx + "sg")
        a1 = act(bscr[:, 2048:4096], raw[:], AF.Abs, deps=deps, tag=pfx + "ab")
        m1 = ts1(bscr[:, 2048:4096], bscr[:, 2048:4096], 0.25, ALU.max,
                 deps=[a1], tag=pfx + "mx")
        return tt(out_t[:], bscr[:, 0:2048], bscr[:, 2048:4096], ALU.mult,
                  deps=[s2, m1], tag=pfx + "ml")

    ev_gaT = gate_tile("ga", garw, gaT, ev_ga)
    ev_gbT = gate_tile("gb", gbrw, gbT0, ev_gb + [ev_gaT])
    ev_st_sb = dma("sync", g["sbcb_hbm"][0:128, :], sbT0[:], deps=ev_sbcb, tag="st_sb")
    ev_st_cb = dma("sync", g["sbcb_hbm"][128:256, :], cbT0[:], deps=ev_sbcb, tag="st_cb")
    ev_st_gb = dma("sync", g["gb_hbm"], gbT0[:], deps=[ev_gbT], tag="st_gb")
    ev_phaseB = [ev_st_sb, ev_st_cb, ev_st_gb, ev_gaT] + ev_sa + ev_ca + ev_repc \
        + [ev_gkm, ev_scol]
    cB.close()
    pp_step(4)

    cK = ExitStack()
    EK = cK.enter_context
    xt = sb_(EK, "xt", 128, D, F32)
    ht = [sb_(EK, "ht0", 128, D, BF16)]
    hfmR = [sb_(EK, "hfmR0", 128, 2048, BF16)]
    hfmQb = sb_(EK, "hfmQb", 128, 16384, BF16)
    qs = sb_(EK, "qs", 128, D, F32)
    qs2 = sb_(EK, "qs2", 128, D, F32)
    qbs = sb_(EK, "qbs", 128, D, BF16)
    ks = sb_(EK, "ks", 128, 512, F32)
    kbs = sb_(EK, "kbs", 128, 512, BF16)
    ropC = sb_(EK, "ropC", 128, 128, F32)
    ropS = sb_(EK, "ropS", 128, 128, F32)

    def ln_tile(pfx, x_ap, out_ht, ca_t, sa_t, ev_x, extra, ca_deps):
        d0 = act(dmp[:], x_ap, AF.Identity, accum=stats[:, 0:1],
                 deps=[ev_x] + extra, tag=pfx + "sx")
        d1 = act(dmp[:], x_ap, AF.Square, accum=stats[:, 1:2], deps=[d0], tag=pfx + "sx2")
        m1 = ts1(stats[:, 2:3], stats[:, 0:1], 1.0 / 2048, ALU.mult, deps=[d0], tag=pfx + "m")
        m2 = ts1(stats[:, 3:4], stats[:, 1:2], 1.0 / 2048, ALU.mult, deps=[d1], tag=pfx + "e2")
        m3 = tt(stats[:, 4:5], stats[:, 2:3], stats[:, 2:3], ALU.mult, deps=[m1], tag=pfx + "mm")
        m4 = tt(stats[:, 4:5], stats[:, 3:4], stats[:, 4:5], ALU.subtract,
                deps=[m2, m3], tag=pfx + "var")
        m5 = ts1(stats[:, 4:5], stats[:, 4:5], 1e-5, ALU.add, deps=[m4], tag=pfx + "veps")
        mr = rsqrt_chain(pfx, stats[:, 4:5], 1, stats[:, 6:10], [m5])
        m6 = ts1(stats[:, 5:6], stats[:, 2:3], -1.0, ALU.mult, deps=[m1], tag=pfx + "negm")
        a0 = ts2(out_ht[:], x_ap, stats[:, 5:6], stats[:, 4:5], ALU.add, ALU.mult,
                 deps=[mr, m6], tag=pfx + "app")
        a1 = tt(out_ht[:], out_ht[:], ca_t[:], ALU.mult, deps=[a0] + ca_deps, tag=pfx + "ca")
        return tt(out_ht[:], out_ht[:], sa_t[:], ALU.add, deps=[a1], tag=pfx + "sa")

    xkv3 = g["x_kv"].rearrange("(s p) c -> s p c", p=128)
    rc3 = g["ropeC_kv"].rearrange("(s p) c -> s p c", p=128)
    rs3 = g["ropeS_kv"].rearrange("(s p) c -> s p c", p=128)
    wk3 = g["wk"].rearrange("(kc p) c -> p kc c", p=128)
    wv3 = g["wv"].rearrange("(kc p) c -> p kc c", p=128)
    ev_wk = add("pool", lambda h: h.dma_start(
        out=arAb[0:128, 0:8192].rearrange("p (kc c) -> p kc c", c=512), in_=wk3),
        dma=True, deps=[aru["A"]], tag="ld_wk")
    ev_wv = add("pool", lambda h: h.dma_start(
        out=arBb[0:128, 0:8192].rearrange("p (kc c) -> p kc c", c=512), in_=wv3),
        dma=True, deps=[aru["B"]], tag="ld_wv")

    use = {"xt": None, "ht": [list(ev_phaseB)], "hfmR": [None], "rope": None,
           "ks": None, "kbs": list(ev_phaseB), "ln": None, "qbs": [],
           "kT_ps": [], "qT_ps": [[], []]}
    kvtp = {0: ev_mod_ev[22], 1: ev_mod_ev[23]}
    ev_kT_all = []
    ev_v_last = None
    ev_knrm_all = []
    lk = lv = None

    for s in range(NKV):
        par = 0
        ph0 = ev_phaseB if s == 0 else []
        evx = dma("sync", xt[:], xkv3[s], deps=[use["xt"]] + ph0, tag=f"kvx{s}")
        evrc = dma("sync", ropC[:], rc3[s], deps=[use["rope"]] + ph0, tag=f"kvrc{s}")
        evrs = dma("sync", ropS[:], rs3[s], deps=[use["rope"]] + ph0, tag=f"kvrs{s}")
        ev_h = ln_tile(f"kl{s}_", xt[:], ht[par], caT, saT, evx,
                       list(use["ht"][par]) + [use["ln"]], ev_ca + ev_sa)
        use["ln"] = ev_h
        use["xt"] = ev_h
        evT = [None] * 16
        for gq_ in range(4):
            bank = pe_ps[gq_ % 2]
            tms = []
            for k2 in range(4):
                fc = 4 * gq_ + k2
                tms.append(mm(bank[:, 128 * k2:128 * (k2 + 1)],
                              ht[par][:, 128 * fc:128 * (fc + 1)], i128b[:],
                              True, True, deps=[ev_h, ev_ib, kvtp.get(gq_ % 2)],
                              tag=f"kvT{s}_{gq_}_{k2}"))
            evg = tcp(hfmR[par][:, 512 * gq_:512 * (gq_ + 1)], bank[:],
                      deps=tms + [use["hfmR"][par]] + ph0, tag=f"kvTe{s}_{gq_}")
            kvtp[gq_ % 2] = evg
            for k2 in range(4):
                evT[4 * gq_ + k2] = evg
        use["ht"][par] = [evT[0], evT[4], evT[8], evT[12]]
        for fc in range(16):
            lk = mm(pe_ps[2][:], hfmR[par][:, 128 * fc:128 * (fc + 1)],
                    arAb[0:128, 512 * fc:512 * (fc + 1)], fc == 0, fc == 15,
                    deps=[evT[fc], ev_wk] + ([use["ks"]] if fc == 0 else []),
                    tag=f"kmm{s}_{fc}")
        for fc in range(16):
            lv = mm(pe_ps[3][:], hfmR[par][:, 128 * fc:128 * (fc + 1)],
                    arBb[0:128, 512 * fc:512 * (fc + 1)], fc == 0, fc == 15,
                    deps=[evT[fc], ev_wv] + ([ev_v_last] if fc == 0 else []),
                    tag=f"vmm{s}_{fc}")
        use["hfmR"][par] = lv
        ev_v_last = act(vtm[:, 512 * s:512 * (s + 1)], pe_ps[3][:], AF.Identity,
                        deps=[lv], tag=f"vev{s}")
        ek0 = act(ks[:], pe_ps[2][:], AF.Identity, deps=[lk] + ph0, tag=f"kev{s}")
        eacc = []
        for gi in range(4):
            eacc.append(act(dmp[:, 128 * gi:128 * (gi + 1)], ks[:, 128 * gi:128 * (gi + 1)],
                            AF.Square, accum=stats[:, 10 + gi:11 + gi],
                            deps=[ek0], tag=f"krms{s}_{gi}"))
        em0 = ts2(stats[:, 10:14], stats[:, 10:14], 1.0 / 128, 1e-6, ALU.mult, ALU.add,
                  deps=eacc, tag=f"kms{s}")
        emr = rsqrt_chain(f"kr{s}_", stats[:, 10:14], 4, stats[:, 14:30], [em0])
        ek1 = []
        for gi in range(4):
            ek1.append(ts1(ks[:, 128 * gi:128 * (gi + 1)], ks[:, 128 * gi:128 * (gi + 1)],
                           stats[:, 10 + gi:11 + gi], ALU.mult, deps=[emr],
                           tag=f"kno{s}_{gi}"))
        ek2 = tt(ks[:], ks[:], gkmB[:], ALU.mult, deps=ek1 + [ev_gkm], tag=f"kgn{s}")
        e_t1 = tt(qs[:, 0:512].rearrange("p (gr d) -> p gr d", d=128),
                  ks[:].rearrange("p (gr d) -> p gr d", d=128),
                  ropC[:].unsqueeze(1).broadcast_to([128, 4, 128]),
                  ALU.mult, deps=[ek2, evrc] + ph0, tag=f"krp1_{s}")
        e_rope = [e_t1]
        for gi in range(4):
            b0 = 128 * gi
            e_rope.append(tt(qs2[:, b0:b0 + 64], ks[:, b0 + 64:b0 + 128], ropS[:, 0:64],
                             ALU.mult, deps=[ek2, evrs] + ph0, tag=f"krp2a{s}_{gi}"))
            e_rope.append(tt(qs2[:, b0 + 64:b0 + 128], ks[:, b0:b0 + 64], ropS[:, 64:128],
                             ALU.mult, deps=[ek2, evrs] + ph0, tag=f"krp2b{s}_{gi}"))
        e_k = tt(kbs[:], qs[:, 0:512], qs2[:, 0:512], ALU.add,
                 deps=e_rope + list(use["kbs"]), tag=f"krpo{s}")
        use["ks"] = e_k
        use["rope"] = e_k
        ekn = []
        for gi in range(4):
            ekn.append(act(dmp[:, 128 * (4 + gi):128 * (5 + gi)],
                           kbs[:, 128 * gi:128 * (gi + 1)], AF.Square,
                           accum=knrm[:, 4 * s + gi:4 * s + gi + 1],
                           deps=[e_k], tag=f"knrm{s}_{gi}"))
        ev_knrm_all += ekn
        ktms = []
        for gi in range(4):
            ktms.append(mm(pe_ps2[0][:, 128 * gi:128 * (gi + 1)],
                           kbs[:, 128 * gi:128 * (gi + 1)], i128b[:], True, True,
                           deps=[e_k, ev_ib] + ekn + use["kT_ps"],
                           tag=f"kT{s}_{gi}"))
        ekT = []
        for gi in range(4):
            ekT.append(tcp(kfm[gi][:, 128 * s:128 * (s + 1)],
                           pe_ps2[0][:, 128 * gi:128 * (gi + 1)], deps=ktms,
                           tag=f"kTe{s}_{gi}"))
        use["kT_ps"] = ekT
        ev_kT_all += ekT
        use["kbs"] = ekT
        pp_step(3)
    aru["A"] = lk
    aru["B"] = lv
    use["qT_ps"][0] = use["kT_ps"]

    e1 = add("dve", lambda h: h.reduce_max(
        stats[:, 10:14], knrm[:].rearrange("p (s gr) -> p gr s", gr=4), axis=AX.X),
        deps=ev_knrm_all, tag="mg_red")
    e2 = mm(small_ps[0:4, 0:128], stats[:, 10:14], i128f[:], True, True,
            deps=[e1, ev_scol_mm[-1]], tag="mg_tp")
    e3 = redmax(stats[0:4, 14:15], small_ps[0:4, 0:128], deps=[e2], tag="mg_red2")
    e3b = act(stats[0:4, 14:15], stats[0:4, 14:15], AF.Sqrt, deps=[e3], tag="mg_sqrt")
    e4 = mm(small_ps[0:1, 128:132], stats[0:4, 14:15], i128f[0:4, 0:4], True, True,
            deps=[e3b], tag="mg_row")
    e5 = act(rowsc[:, 0:4], small_ps[0:1, 128:132], AF.Identity, deps=[e4], tag="mg_rowev")
    e6 = mm(small_ps[:, 0:4], ones[0:1, :], rowsc[0:1, 0:4], True, True,
            deps=[e5], tag="mg_bc")
    ev_mgb = act(mgb[:], small_ps[:, 0:4], AF.Identity, deps=[e6], tag="mg_ev")

    xq3 = g["x_q"].rearrange("(s p) c -> s p c", p=128)
    rcq3 = g["ropeC_q"].rearrange("(s p) c -> s p c", p=128)
    rsq3 = g["ropeS_q"].rearrange("(s p) c -> s p c", p=128)
    wq4 = g["wq"].rearrange("(kc p) c -> p kc c", p=128)
    ev_cmax_last = None
    ev_qT_all = []
    ev_qmm = {}

    for oh in range(2):
        hh0 = 8 * oh
        depA = [aru["A"]] if oh == 0 else [ev_qmm[(0, 7)]]
        depB = [aru["B"]] if oh == 0 else [ev_qmm[(0, 7)]]
        evA = add("pool", lambda h, oh=oh: h.dma_start(
            out=arAb[0:128, 0:8192].rearrange("p (kc c) -> p kc c", c=1024),
            in_=wq4[:, 0:8, 1024 * oh:1024 * (oh + 1)]),
            dma=True, deps=depA, tag=f"ldwqA{oh}")
        evB = add("pool", lambda h, oh=oh: h.dma_start(
            out=arBb[0:128, 0:8192].rearrange("p (kc c) -> p kc c", c=1024),
            in_=wq4[:, 8:16, 1024 * oh:1024 * (oh + 1)]),
            dma=True, deps=depB, tag=f"ldwqB{oh}")
        for tq in range(8):
            qt = tq // 4
            par = 0
            if oh == 0:
                evx = dma("sync", xt[:], xq3[tq], deps=[use["xt"]], tag=f"qx{tq}")
                ev_h = ln_tile(f"ql{tq}_", xt[:], ht[par], caT, saT, evx,
                               list(use["ht"][par]) + [use["ln"]], [])
                use["ln"] = ev_h
                evT = [None] * 16
                for gq_ in range(4):
                    bank = pe_ps[gq_ % 2]
                    tms = []
                    for k2 in range(4):
                        fc = 4 * gq_ + k2
                        tms.append(mm(bank[:, 128 * k2:128 * (k2 + 1)],
                                      ht[par][:, 128 * fc:128 * (fc + 1)], i128b[:],
                                      True, True,
                                      deps=[ev_h, ev_ib, kvtp.get(gq_ % 2)],
                                      tag=f"qT{tq}_{gq_}_{k2}"))
                    evg = tcp(hfmQb[:, 2048 * tq + 512 * gq_:
                                    2048 * tq + 512 * (gq_ + 1)], bank[:],
                              deps=tms, tag=f"qTe{tq}_{gq_}")
                    kvtp[gq_ % 2] = evg
                    for k2 in range(4):
                        evT[4 * gq_ + k2] = evg
                use["ht"][par] = [evT[0], evT[4], evT[8], evT[12]]
                dep_h = evT
            else:
                dep_h = [ev_qmm[(0, tq)]]
            evrc = dma("sync", ropC[:], rcq3[tq], deps=[use["rope"]], tag=f"qrc{oh}_{tq}")
            evrs = dma("sync", ropS[:], rsq3[tq], deps=[use["rope"]], tag=f"qrs{oh}_{tq}")
            lq = [None, None]
            for fc in range(16):
                arb = arAb if fc < 8 else arBb
                base = 1024 * (fc % 8)
                for j in range(2):
                    dep = [dep_h[fc] if oh == 0 else dep_h[0],
                           evA if fc < 8 else evB]
                    if fc == 0 and j == 0:
                        dep.append(ev_qmm.get((oh, tq - 1)))
                        if oh == 0 and tq == 0:
                            dep.append(use["ks"])
                    lq[j] = mm(pe_ps[2 + j][:],
                               hfmQb[:, 2048 * tq + 128 * fc:2048 * tq + 128 * (fc + 1)],
                               arb[0:128, base + 512 * j:base + 512 * (j + 1)],
                               fc == 0, fc == 15, deps=dep, tag=f"qmm{oh}_{tq}_{fc}_{j}")
            ev_q0 = act(qs[:, 1024 * oh:1024 * oh + 512], pe_ps[2][:], AF.Identity,
                        deps=[lq[0]], tag=f"qev{oh}_{tq}_0")
            ev_q1 = act(qs[:, 1024 * oh + 512:1024 * (oh + 1)], pe_ps[3][:], AF.Identity,
                        deps=[lq[1]], tag=f"qev{oh}_{tq}_1")
            ev_qmm[(oh, tq)] = ev_q1
            ea = []
            for hh in range(8):
                b0 = 1024 * oh + 128 * hh
                ea.append(act(dmp[:, 128 * hh:128 * (hh + 1)], qs[:, b0:b0 + 128],
                              AF.Square, accum=msq[:, hh0 + hh:hh0 + hh + 1],
                              deps=[ev_q0, ev_q1], tag=f"qrms{oh}_{tq}_{hh}"))
            em0 = ts2(msq[:, hh0:hh0 + 8], msq[:, hh0:hh0 + 8], 1.0 / 128, 1e-6,
                      ALU.mult, ALU.add, deps=ea, tag=f"qms{oh}_{tq}")
            emr = rsqrt_chain(f"qr{oh}_{tq}_", msq[:, hh0:hh0 + 8], 8, stats[:, 0:32], [em0])
            eb = []
            for hh in range(8):
                b0 = 1024 * oh + 128 * hh
                eb.append(ts1(qs[:, b0:b0 + 128], qs[:, b0:b0 + 128],
                              msq[:, hh0 + hh:hh0 + hh + 1], ALU.mult,
                              deps=[emr], tag=f"qno{oh}_{tq}_{hh}"))
            ec = tt(qs[:, 1024 * oh:1024 * (oh + 1)], qs[:, 1024 * oh:1024 * (oh + 1)],
                    gqmB[:, 1024 * oh:1024 * (oh + 1)], ALU.mult, deps=eb,
                    tag=f"qgn{oh}_{tq}")
            ed = []
            for hh in range(8):
                b0 = 1024 * oh + 128 * hh
                ed.append(act(dmp[:, 128 * (8 + hh):128 * (9 + hh)], qs[:, b0:b0 + 128],
                              AF.Square, accum=msq[:, hh0 + hh:hh0 + hh + 1],
                              deps=[ec], tag=f"qn2{oh}_{tq}_{hh}"))
            e_t1 = tt(qs2[:, 1024 * oh:1024 * (oh + 1)].rearrange("p (hh d) -> p hh d", d=128),
                      qs[:, 1024 * oh:1024 * (oh + 1)].rearrange("p (hh d) -> p hh d", d=128),
                      ropC[:].unsqueeze(1).broadcast_to([128, 8, 128]),
                      ALU.mult, deps=[evrc] + ed, tag=f"qrp1_{oh}_{tq}")
            e_rope = [e_t1]
            for hh in range(8):
                b0 = 1024 * oh + 128 * hh
                e_rope.append(tt(xt[:, b0:b0 + 64], qs[:, b0 + 64:b0 + 128],
                                 ropS[:, 0:64], ALU.mult,
                                 deps=[evrs, use["ln"]] + ed, tag=f"qrp2a{oh}_{tq}_{hh}"))
                e_rope.append(tt(xt[:, b0 + 64:b0 + 128], qs[:, b0:b0 + 64],
                                 ropS[:, 64:128], ALU.mult,
                                 deps=[evrs] + ed, tag=f"qrp2b{oh}_{tq}_{hh}"))
            e_q = tt(qbs[:, 1024 * oh:1024 * (oh + 1)], qs2[:, 1024 * oh:1024 * (oh + 1)],
                     xt[:, 1024 * oh:1024 * (oh + 1)], ALU.add,
                     deps=e_rope + list(use["qbs"]), tag=f"qrpo{oh}_{tq}")
            use["rope"] = e_q
            use["xt"] = e_q
            es = act(msq[:, hh0:hh0 + 8], msq[:, hh0:hh0 + 8], AF.Sqrt,
                     deps=ed + [ev_mgb], tag=f"csq{oh}_{tq}")
            eM = tt(msq[:, hh0:hh0 + 8].rearrange("p (a b) -> p a b", b=4),
                    msq[:, hh0:hh0 + 8].rearrange("p (a b) -> p a b", b=4),
                    mgb[:, 2 * oh:2 * oh + 2].unsqueeze(2).broadcast_to([128, 2, 4]),
                    ALU.mult, deps=[es], tag=f"cmg{oh}_{tq}")
            if tq % 4 == 0:
                ecm = tcp(cmax[:, 16 * qt + hh0:16 * qt + hh0 + 8], msq[:, hh0:hh0 + 8],
                          deps=[eM, ev_cmax_last], tag=f"cmax0_{oh}_{tq}")
            else:
                ecm = tt(cmax[:, 16 * qt + hh0:16 * qt + hh0 + 8],
                         cmax[:, 16 * qt + hh0:16 * qt + hh0 + 8],
                         msq[:, hh0:hh0 + 8], ALU.max,
                         deps=[eM, ev_cmax_last], tag=f"cmaxm_{oh}_{tq}")
            ev_cmax_last = ecm
            eqT = []
            for hb in range(2):
                bank = pe_ps2[hb]
                tms = []
                for k2 in range(4):
                    hh = 4 * hb + k2
                    b0 = 1024 * oh + 128 * hh
                    tms.append(mm(bank[:, 128 * k2:128 * (k2 + 1)],
                                  qbs[:, b0:b0 + 128], i128b[:], True, True,
                                  deps=[e_q, ev_ib] + use["qT_ps"][hb],
                                  tag=f"qfT{oh}_{tq}_{hh}"))
                evs_ = []
                for k2 in range(4):
                    hh = 4 * hb + k2
                    hg = hh0 + hh
                    evs_.append(tcp(qfm[:, 1024 * hg + 512 * qt + 128 * (tq % 4):
                                        1024 * hg + 512 * qt + 128 * (tq % 4) + 128],
                                    bank[:, 128 * k2:128 * (k2 + 1)], deps=tms,
                                    tag=f"qfTe{oh}_{tq}_{hh}"))
                use["qT_ps"][hb] = evs_
                eqT += evs_
            ev_qT_all += eqT
            use["qbs"] = eqT
            pp_step(3)
    aru["A"] = ev_qmm[(1, 7)]
    aru["B"] = ev_qmm[(1, 7)]
    cK.close()

    cA = ExitStack()
    EA = cA.enter_context
    ofm = sb_(EA, "ofm", 128, 8192, BF16)
    cbias = sb_(EA, "cbias", 128, 512, F32)
    cbrow = sb_(EA, "cbrow", 1, 256, F32)
    pbuf = [sb_(EA, f"pbuf{i}", 128, 512, BF16) for i in range(2)]
    wscr = sb_(EA, "wscr", 128, D, F32)
    wscr2 = sb_(EA, "wscr2", 128, D, F32)
    rbS = sb_(EA, "rbS", 128, 512, F32)
    xtw = sb_(EA, "xtw", 128, D, F32)
    att_sc = [pe_ps[0], pe_ps[1]]
    att_o = pe_ps[2]
    att_sm = pe_ps[3]

    en1 = mm(small_ps[0:32, 0:128], cmax[:], i128f[:], True, True,
             deps=[ev_cmax_last, e6], tag="nc_tp")
    en2 = redmax(stats[0:32, 0:1], small_ps[0:32, 0:128], deps=[en1], tag="nc_red")
    en3 = mm(small_ps[0:1, 128:160], stats[0:32, 0:1], i128f[0:32, 0:32], True, True,
             deps=[en2], tag="nc_row")
    en4 = act(rowsc[:, 32:64], small_ps[0:1, 128:160], AF.Identity, deps=[en3],
              tag="nc_rowev")
    ev_cb = []
    for qt in range(2):
        a_ap = rowsc[0:1, 32 + 16 * qt:48 + 16 * qt].unsqueeze(2).broadcast_to([1, 16, 16])
        b_ap = deadr[0:1, 16 * qt:16 * (qt + 1)].unsqueeze(1).broadcast_to([1, 16, 16])
        ecr = add("dve", lambda h, a_ap=a_ap, b_ap=b_ap: h.scalar_tensor_tensor(
            cbrow[0:1, 0:256].rearrange("r (a b) -> r a b", b=16),
            a_ap, -1.0, b_ap, ALU.mult, ALU.add),
            deps=[en4] + evC + ([ev_cb[-1]] if ev_cb else []), tag=f"cbrow{qt}")
        eb1 = mm(small_ps[:, 0:256], ones[0:1, :], cbrow[0:1, 0:256], True, True,
                 deps=[ecr, en3], tag=f"cbmm{qt}")
        ev_cb.append(act(cbias[:, 256 * qt:256 * (qt + 1)], small_ps[:, 0:256],
                         AF.Identity, deps=[eb1], tag=f"cbev{qt}"))

    wo4 = g["wo"].rearrange("(dc p) c -> p dc c", p=128)
    ev_x1_st = []
    ev_wo_ev = {}
    ev_oe = {}
    for qt in range(2):
        slots = _slots(qt)
        for hg in range(16):
            gi = hg // 4
            last_i = len(slots) - 1
            ev_exp = [None, None]
            ev_pv = None
            ev_smm = None
            exs = [None] * len(slots)
            pvs = [None] * len(slots)
            sms = [None] * len(slots)
            for si in range(len(slots) + 1):
                if si < len(slots):
                    kind, s, qoff, ncols = slots[si]
                    par = si % 2
                    dep_sc = [ev_exp[par]]
                    if (qt, hg) == (0, 0) and si <= 1:
                        dep_sc += ev_kT_all + ev_qT_all
                    if si <= 1:
                        dep_sc += [ev_oe.get((qt, hg - 1)), ev_cb[-1]]
                        if hg == 0 and ev_x1_st:
                            dep_sc.append(ev_x1_st[-1])
                    sc = mm(att_sc[par][:, 0:ncols], kfm[gi][:, 128 * s:128 * (s + 1)],
                            qfm[:, 1024 * hg + 512 * qt + qoff:
                                1024 * hg + 512 * qt + qoff + ncols],
                            True, True, deps=dep_sc, tag=f"sc{qt}_{hg}_{si}")
                    dep_e = [sc]
                    if kind == "C":
                        mska = tt(att_sc[par][:, 0:128], att_sc[par][:, 0:128],
                                  dmask[:], ALU.add, deps=[sc],
                                  tag=f"msk{qt}_{hg}_{si}")
                        dep_e = [mska]
                    bcol = cbias[:, 256 * qt + 16 * hg + si:256 * qt + 16 * hg + si + 1]
                    if si >= 2:
                        dep_e += [pvs[si - 2], sms[si - 2]]
                    ex = act(pbuf[par][:, 0:ncols], att_sc[par][:, 0:ncols], AF.Exp,
                             bias=bcol, deps=dep_e + [ev_cb[qt]],
                             tag=f"ex{qt}_{hg}_{si}")
                    ev_exp[par] = ex
                    exs[si] = ex
                if si >= 1:
                    sj = si - 1
                    kind1, s1, qoff1, ncols1 = slots[sj]
                    par1 = sj % 2
                    ev_pv = mm(att_o[:, qoff1:qoff1 + ncols1],
                               vtm[:, 512 * s1 + 128 * gi:512 * s1 + 128 * (gi + 1)],
                               pbuf[par1][:, 0:ncols1], sj == 0, sj == last_i,
                               deps=[exs[sj], ev_v_last, ev_pv],
                               tag=f"pv{qt}_{hg}_{sj}")
                    pvs[sj] = ev_pv
                    ev_smm = mm(att_sm[0:1, qoff1:qoff1 + ncols1], onesb[:, 0:1],
                                pbuf[par1][:, 0:ncols1], sj == 0, sj == last_i,
                                deps=[exs[sj], ev_onesb, ev_smm],
                                tag=f"sm{qt}_{hg}_{sj}")
                    sms[sj] = ev_smm
            err = recip(sprow[0:1, 0:512], att_sm[0:1, 0:512], deps=[ev_smm],
                        tag=f"rr{qt}_{hg}")
            ebc = mm(small_ps[:, 0:512], ones[0:1, :], sprow[0:1, 0:512], True, True,
                     deps=[err, ev_oe.get((qt, hg - 1)),
                           (ev_cb[-1] if (qt, hg) == (0, 0) else None)],
                     tag=f"rb{qt}_{hg}")
            erbe = act(rbS[:], small_ps[:, 0:512], AF.Identity,
                       deps=[ebc, ev_oe.get((qt, hg - 1))], tag=f"rbe{qt}_{hg}")
            ev_oe[(qt, hg)] = tt(ofm[:, 512 * hg:512 * (hg + 1)], att_o[:],
                                 rbS[:], ALU.mult,
                                 deps=[erbe, ev_pv], tag=f"oev{qt}_{hg}")
            ev_att_done = [ev_oe[(qt, hg)], err, erbe, ev_exp[0], ev_exp[1],
                           ev_pv, ev_smm]
            pp_step(4)

        pe8 = pe_ps + pe_ps2 + [small_ps]
        for pr in range(2):
            lwA = None
            lwB = None
            evac_pair = []
            for hp in range(2):
                eldA = add("pool", lambda h, hp=hp: h.dma_start(
                    out=arAb[0:128, 0:8192].rearrange("p (dc c) -> p dc c", c=2048),
                    in_=wo4[:, 8 * hp:8 * hp + 4, :]), dma=True,
                    deps=[aru["A"]], tag=f"ldwoA{qt}_{pr}_{hp}")
                eldB = add("pool", lambda h, hp=hp: h.dma_start(
                    out=arBb[0:128, 0:8192].rearrange("p (dc c) -> p dc c", c=2048),
                    in_=wo4[:, 8 * hp + 4:8 * hp + 8, :]), dma=True,
                    deps=[aru["B"]], tag=f"ldwoB{qt}_{pr}_{hp}")
                for ti in range(2):
                    tloc = 2 * pr + ti
                    for dc8 in range(8):
                        dcg = 8 * hp + dc8
                        arb = arAb if dc8 < 4 else arBb
                        base = 2048 * (dc8 % 4)
                        for j in range(4):
                            dep = [eldA if dc8 < 4 else eldB, ev_oe[(qt, dcg)]]
                            if hp == 0 and dc8 == 0:
                                dep.append(ev_wo_ev.get((4 * ti + j,)))
                                if pr == 0:
                                    dep += ev_att_done
                            lw = mm(pe8[4 * ti + j][:],
                                    ofm[:, 512 * dcg + 128 * tloc:
                                        512 * dcg + 128 * tloc + 128],
                                    arb[0:128, base + 512 * j:base + 512 * (j + 1)],
                                    hp == 0 and dc8 == 0, hp == 1 and dc8 == 7,
                                    deps=dep, tag=f"wo{qt}_{pr}_{hp}_{tloc}_{dc8}_{j}")
                            if dc8 < 4:
                                lwA = lw
                            else:
                                lwB = lw
                aru["A"] = lwA
                aru["B"] = lwB
            for ti in range(2):
                tloc = 2 * pr + ti
                tq = 4 * qt + tloc
                wdst = wscr if ti == 0 else wscr2
                evs = [act(wdst[:, 512 * j:512 * (j + 1)], pe8[4 * ti + j][:],
                           AF.Identity, deps=[lwA, lwB,
                           (ev_x1_st[-1] if ev_x1_st else None)],
                           tag=f"wpe{qt}_{pr}_{ti}_{j}") for j in range(4)]
                for j in range(4):
                    ev_wo_ev[(4 * ti + j,)] = evs[j]
                eg = tt(wdst[:], wdst[:], gaT[:], ALU.mult,
                        deps=evs + [ev_gaT], tag=f"wga{qt}_{pr}_{ti}")
                exl = dma("sync", xtw[:], xq3[tq],
                          deps=[eg, (ev_x1_st[-1] if ev_x1_st else None)],
                          tag=f"wxl{qt}_{pr}_{ti}")
                ex1 = tt(xtw[:], xtw[:], wdst[:], ALU.add, deps=[exl, eg],
                         tag=f"wx1{qt}_{pr}_{ti}")
                ev_x1_st.append(dma("sync", g["x1_hbm"][128 * tq:128 * (tq + 1), :],
                                    xtw[:], deps=[ex1], tag=f"x1st{qt}_{pr}_{ti}"))
    while pp["i"] < len(pp_list):
        pp_step(8)
    cA.close()
    c1.close()

    def scal_bcast(pfx, wc, colH, colNH, colPH, denom, deps):
        e1 = mm(small_ps[0:1, 0:1], wacc[:, wc:wc + 1], ones[:, 0:1], True, True,
                deps=deps + [en3], tag=pfx + "sum")
        e2 = act(rowsc[:, 8 + wc:9 + wc], small_ps[0:1, 0:1], AF.Identity,
                 deps=[e1], tag=pfx + "row")
        e3 = mm(small_ps[:, 0:1], ones[0:1, :], rowsc[0:1, 8 + wc:9 + wc], True, True,
                deps=[e2], tag=pfx + "bc")
        e4 = act(scl[:, colH:colH + 1], small_ps[:, 0:1], AF.Identity, deps=[e3],
                 tag=pfx + "ev")
        e5 = ts1(scl[:, colH:colH + 1], scl[:, colH:colH + 1], 0.5 / denom, ALU.mult,
                 deps=[e4], tag=pfx + "h")
        e6 = ts1(scl[:, colNH:colNH + 1], scl[:, colH:colH + 1], -1.0, ALU.mult,
                 deps=[e5], tag=pfx + "nh")
        e7 = tcp(scl[:, colPH:colPH + 1], scl[:, colH:colH + 1], deps=[e5], tag=pfx + "ph")
        return [e5, e6, e7]

    ev_sg = scal_bcast("sg", 0, S_HG, S_NHG, S_PHG, D * I,
                       [pp["last"]["wgT"]] + [ev_wo_ev[(j,)] for j in range(8)])
    ev_su = scal_bcast("su", 1, S_HU, S_NHU, S_PHU, D * I, [pp["last"]["wuT"]] + ev_sg)
    ev_sd = scal_bcast("sd", 2, S_HD, S_NHD, S_PHD, D * I, [pp["last"]["wdT"]] + ev_su)
    ek2a = ts1(scl[:, S_K2:S_K2 + 1], scl[:, S_HU:S_HU + 1],
               float(np.float32(1.0 / np.sqrt(2.0))),
               ALU.mult, deps=ev_su, tag="k2")
    ek2b = ts1(scl[:, S_K2_127:S_K2_127 + 1], scl[:, S_K2:S_K2 + 1], 127.0, ALU.mult,
               deps=[ek2a], tag="k2b")
    elam = ts1(scl[:, S_LAMD:S_LAMD + 1], scl[:, S_HD:S_HD + 1], 2.0, ALU.mult,
               deps=ev_sd, tag="lamd")

    cxq = ExitStack()
    Exq = cxq.enter_context
    xq = [sb_(Exq, f"xq{fc}", 128, 1024, BF16) for fc in range(16)]
    c2 = ExitStack()
    E2 = c2.enter_context
    sbT = sb_(E2, "sbT", 128, D, BF16)
    cbT = sb_(E2, "cbT", 128, D, BF16)
    xt2 = [sb_(E2, "xt2a", 128, D, F32), sb_(E2, "xt2b", 128, D, F32)]
    h2t = [sb_(E2, f"h2t{i}", 128, D, BF16) for i in range(2)]
    h2fm = [sb_(E2, f"h2fm{i}", 128, 2048, BF16) for i in range(2)]
    f1 = [sb_(E2, f"f1_{fc}", 128, 1024, F32) for fc in range(16)]
    Mq = sb_(E2, "Mq", 128, 1024, F32)
    tscr = sb_(E2, "tscr", 128, 1024, F32)
    z1s = sb_(E2, "z1s", 128, 1024, F32)
    z2s = sb_(E2, "z2s", 128, 1024, BF16)
    sBt = sb_(E2, "sBt", 128, 1024, F32)
    rBt = sb_(E2, "rBt", 128, 1024, F32)
    st2l = [sb_(E2, "st2a", 128, 32, F32), sb_(E2, "st2b", 128, 32, F32)]
    dmpl = [sb_(E2, "dmp2a", 128, 2048, BF16), sb_(E2, "dmp2b", 128, 2048, BF16)]
    f_ps = [ps_(E2, "f_ps0", 128, 512), ps_(E2, "f_ps1", 128, 512)]
    tp_ps = [ps_(E2, "tp_ps0", 128, 512), ps_(E2, "tp_ps1", 128, 512)]
    sm2_ps = ps_(E2, "sm2_ps", 128, 512)

    ev_sbT = dma("sync", sbT[:], g["sbcb_hbm"][0:128, :], deps=[ev_st_sb], tag="ld_sbT")
    ev_cbT = dma("sync", cbT[:], g["sbcb_hbm"][128:256, :], deps=[ev_st_cb], tag="ld_cbT")

    def ln2_tile(pfx, x_ap, out_ht, ev_x, extra, lane=0):
        st2 = st2l[lane]
        dmp = dmpl[lane]
        d0 = act(dmp[:], x_ap, AF.Identity, accum=st2[:, 0:1],
                 deps=[ev_x] + extra, tag=pfx + "sx")
        d1 = act(dmp[:], x_ap, AF.Square, accum=st2[:, 1:2], deps=[d0], tag=pfx + "sx2")
        m1 = ts1(st2[:, 2:3], st2[:, 0:1], 1.0 / 2048, ALU.mult, deps=[d0], tag=pfx + "m")
        m2 = ts1(st2[:, 3:4], st2[:, 1:2], 1.0 / 2048, ALU.mult, deps=[d1], tag=pfx + "e2")
        m3 = tt(st2[:, 4:5], st2[:, 2:3], st2[:, 2:3], ALU.mult, deps=[m1], tag=pfx + "mm")
        m4 = tt(st2[:, 4:5], st2[:, 3:4], st2[:, 4:5], ALU.subtract, deps=[m2, m3],
                tag=pfx + "var")
        m5 = ts1(st2[:, 4:5], st2[:, 4:5], 1e-5, ALU.add, deps=[m4], tag=pfx + "veps")
        mr = rsqrt_chain(pfx, st2[:, 4:5], 1, st2[:, 6:10], [m5])
        m6 = ts1(st2[:, 5:6], st2[:, 2:3], -1.0, ALU.mult, deps=[m1], tag=pfx + "negm")
        a0 = ts2(out_ht[:], x_ap, st2[:, 5:6], st2[:, 4:5], ALU.add, ALU.mult,
                 deps=[mr, m6], tag=pfx + "app")
        a1 = tt(out_ht[:], out_ht[:], cbT[:], ALU.mult, deps=[a0, ev_cbT], tag=pfx + "cb")
        return tt(out_ht[:], out_ht[:], sbT[:], ALU.add, deps=[a1, ev_sbT], tag=pfx + "sb")

    use2 = {"xt2": [None, None], "h2t": [None, None], "h2fm": [None, None],
            "ln": [None, None]}
    ev_f1 = {}
    tpuse = {}
    for tq in range(8):
        par = tq % 2
        evx = dma("sync", xt2[par][:], g["x1_hbm"][128 * tq:128 * (tq + 1), :],
                  deps=[use2["xt2"][par], ev_x1_st[tq]], tag=f"l2x{tq}")
        ev_h = ln2_tile(f"l2_{tq}_", xt2[par][:], h2t[par], evx,
                        [use2["h2t"][par], use2["ln"][par]], lane=par)
        use2["ln"][par] = ev_h
        use2["xt2"][par] = ev_h
        evT = [None] * 4
        lastf = None
        for step in range(6):
            if step < 4:
                gq_ = step
                bank = tp_ps[gq_ % 2]
                tms = []
                for k in range(4):
                    fc = 4 * gq_ + k
                    tms.append(mm(bank[:, 128 * k:128 * (k + 1)],
                                  h2t[par][:, 128 * fc:128 * (fc + 1)], i128b[:],
                                  True, True,
                                  deps=[ev_h, ev_ib, tpuse.get(gq_ % 2)],
                                  tag=f"l2T{tq}_{gq_}_{k}"))
                evT[gq_] = tcp(h2fm[par][:, 512 * gq_:512 * (gq_ + 1)], bank[:],
                               deps=tms + [ev_f1.get((tq - 2, 4 * gq_ + k))
                                           for k in range(4)],
                               tag=f"l2Te{tq}_{gq_}")
                tpuse[gq_ % 2] = evT[gq_]
            if step >= 2:
                gq2 = step - 2
                for k in range(4):
                    fc2 = 4 * gq2 + k
                    pj = fc2 % 2
                    ef = mm(f_ps[pj][:, 0:128], h128[:],
                            h2fm[par][:, 128 * fc2:128 * (fc2 + 1)], True, True,
                            deps=[evT[gq2], evC[3], ev_f1.get((tq, fc2 - 2))],
                            tag=f"f1mm{tq}_{fc2}")
                    ev_f1[(tq, fc2)] = act(f1[fc2][:, 128 * tq:128 * (tq + 1)],
                                           f_ps[pj][:, 0:128], AF.Identity,
                                           deps=[ef], tag=f"f1ev{tq}_{fc2}")
                    lastf = ev_f1[(tq, fc2)]
        use2["h2t"][par] = evT[3]
        use2["h2fm"][par] = lastf

    ev_f1_all = list(ev_f1.values())
    ev_mix = {fc: [ev_f1[(tq, fc)] for tq in range(8)] for fc in range(16)}
    for st in range(4):
        hstep = 1 << st
        newev = {}
        for base in range(0, 16, 2 * hstep):
            for off in range(hstep):
                a = base + off
                b = a + hstep
                e1m = tt(f1[a][:], f1[a][:], f1[b][:], ALU.add,
                         deps=ev_mix[a] + ev_mix[b], tag=f"mx1_{st}_{a}")
                e2m = stt(f1[b][:], f1[b][:], -2.0, f1[a][:], ALU.mult, ALU.add,
                          deps=[e1m], tag=f"mx2_{st}_{b}")
                newev[a] = [e2m]
                newev[b] = [e2m]
        ev_mix = newev

    eM = act(Mq[:], f1[0][:], AF.Abs, deps=ev_mix[0], tag="q1_abs0")
    for fc in range(1, 16):
        ea = act(tscr[:], f1[fc][:], AF.Abs, deps=ev_mix[fc] + [eM], tag=f"q1_abs{fc}")
        eM = tt(Mq[:], Mq[:], tscr[:], ALU.max, deps=[ea], tag=f"q1_max{fc}")
    ev_row = []
    for tb in range(8):
        t1 = mm(sm2_ps[:, 0:128], Mq[:, 128 * tb:128 * (tb + 1)], i128f[:], True, True,
                deps=[eM] + ([ev_row[-1]] if ev_row else []), tag=f"q1tp{tb}")
        t2 = redmax(st2l[0][:, 10:11], sm2_ps[:, 0:128], deps=[t1], tag=f"q1rm{tb}")
        t3 = mm(sm2_ps[0:1, 128:256], st2l[0][:, 10:11], i128f[:], True, True,
                deps=[t2], tag=f"q1rw{tb}")
        ev_row.append(act(sprow[:, 128 * tb:128 * (tb + 1)], sm2_ps[0:1, 128:256],
                          AF.Identity, deps=[t3], tag=f"q1re{tb}"))
    ecl = ts2(sprow[0:1, 0:1024], sprow[0:1, 0:1024],
              float(np.float32(np.sqrt(2.0))), 1e-5, ALU.mult, ALU.max,
              deps=ev_row, tag="q1cl")
    erc = recip(z1s[0:1, 0:1024], sprow[0:1, 0:1024], deps=[ecl], tag="q1rc")
    ev_sB = []
    for j in range(2):
        b1 = mm(sm2_ps[:, 0:512], ones[0:1, :], sprow[0:1, 512 * j:512 * (j + 1)],
                True, True, deps=[ecl, en3 if j == 0 else ev_sB[-1],
                                  (ev_row[-1] if j == 0 else None)], tag=f"q1rbm{j}")
        e_rb = act(rBt[:, 512 * j:512 * (j + 1)], sm2_ps[:, 0:512], AF.Identity,
                   deps=[b1], tag=f"q1rbe{j}")
        b2 = mm(sm2_ps[:, 0:512], ones[0:1, :], z1s[0:1, 512 * j:512 * (j + 1)],
                True, True, deps=[erc, e_rb], tag=f"q1sbm{j}")
        e_sb = act(sBt[:, 512 * j:512 * (j + 1)], sm2_ps[:, 0:512], AF.Identity,
                   deps=[b2], tag=f"q1sbe{j}")
        ev_sB += [e_rb, e_sb]
    esc1 = ts1(rBt[:], rBt[:], 1.0 / 127.0, ALU.mult, deps=ev_sB, tag="q1rbs")
    esc2 = ts1(sBt[:], sBt[:], SQ2_127, ALU.mult, deps=ev_sB, tag="q1sbs")

    def pq1(out, a, b, op, deps=(), tag=""):
        return add("pool", lambda h, o=out, a=a, b=b, op=op:
                   h.tensor_tensor(o, a, b, op), deps=deps, tag=tag)

    ev_xq = {}
    uq1 = {"t": None, "z1": None, "z2": None}
    for fc in range(16):
        e_t = pq1(tscr[:], f1[fc][:], sBt[:], ALU.mult,
                  deps=ev_mix[fc] + [esc2, uq1["t"]], tag=f"q1t{fc}")
        ez1 = act(z1s[:], tscr[:], AF.Abs, deps=[e_t, erc, uq1["z1"]],
                  tag=f"q1z1_{fc}")
        for ci, cf in enumerate(C_FOLDS):
            ez1 = act(z1s[:], z1s[:], AF.Abs, bias=cc(cf), deps=[ez1],
                      tag=f"q1f1_{fc}_{ci}")
        d0 = ts1(z2s[:], tscr[:], -0.25, ALU.add, deps=[e_t, uq1["z2"]],
                 tag=f"q1d0_{fc}")
        ez2 = stt(z2s[:], z2s[:], -1.0, z2s[:], ALU.mult, ALU.max, deps=[d0],
                  tag=f"q1da_{fc}")
        for ci in range(8):
            vv = float(-(2.0 ** (6 - ci)) if ci < 7 else -0.5)
            f0 = ts1(z2s[:], z2s[:], vv, ALU.add, deps=[ez2], tag=f"q1df{fc}_{ci}")
            ez2 = stt(z2s[:], z2s[:], -1.0, z2s[:], ALU.mult, ALU.max, deps=[f0],
                      tag=f"q1dg{fc}_{ci}")
        eg = ts1(z2s[:], z2s[:], 0.25, ALU.is_ge, deps=[ez2], tag=f"q1g{fc}")
        esg = ts2(z2s[:], z2s[:], 2.0, -1.0, ALU.mult, ALU.add, deps=[eg], tag=f"q1sg{fc}")
        ed1 = ts2(z1s[:], z1s[:], -1.0, 0.5, ALU.mult, ALU.add, deps=[ez1], tag=f"q1d1{fc}")
        ee = pq1(z1s[:], z1s[:], z2s[:], ALU.mult, deps=[ed1, esg], tag=f"q1e{fc}")
        eq = pq1(tscr[:], tscr[:], z1s[:], ALU.subtract, deps=[ee], tag=f"q1q{fc}")
        ev_xq[fc] = pq1(xq[fc][:], tscr[:], rBt[:], ALU.mult, deps=[eq, esc1],
                        tag=f"q1xq{fc}")
        uq1["t"] = ev_xq[fc]
        uq1["z1"] = eq
        uq1["z2"] = ee
    c2.close()

    c3 = ExitStack()
    E3 = c3.enter_context
    terng = [sb_(E3, f"terng{i}", 128, 2048, BF16) for i in range(2)]
    ternu = [sb_(E3, f"ternu{i}", 128, 2048, BF16) for i in range(2)]
    s2scr = sb_(E3, "s2scr", 128, 2048, BF16)
    sgt = sb_(E3, "sgt", 128, 512, BF16)
    zt = sb_(E3, "zt", 128, 1024, BF16)
    gu_ps = [ps_(E3, f"gu_ps{j}", 128, 512) for j in range(8)]

    wgT3 = g["wgT"].rearrange("(fc p) o -> p fc o", p=128)
    wuT3 = g["wuT"].rearrange("(fc p) o -> p fc o", p=128)
    arq = {0: ("A", arA, arAb), 1: ("B", arB, arBb)}
    ev_tern_use = {"g": [None, None], "u": [None, None]}
    ev_gu = {}
    ev_zst = {}
    ev_sil_rd = {}
    ev_zz_rd = {}
    for c in range(64):
        cpar = c % 2
        nmA, arf, arb_ = arq[cpar]
        elg = dma("sync", arf[:, 0:2048].rearrange("p (fc o) -> p fc o", o=128),
                  wgT3[:, :, 128 * c:128 * (c + 1)],
                  deps=[aru[nmA]], tag=f"gld{c}")
        elu = dma("sync", arf[:, 2048:4096].rearrange("p (fc o) -> p fc o", o=128),
                  wuT3[:, :, 128 * c:128 * (c + 1)],
                  deps=[aru[nmA]], tag=f"uld{c}")
        eg1 = act(terng[cpar][:], arf[:, 0:2048], AF.Sign, bias=scl[:, S_NHG:S_NHG + 1],
                  deps=[elg, ev_tern_use["g"][cpar]] + ev_sg, tag=f"tg1_{c}")
        eg2 = act(s2scr[:], arf[:, 0:2048], AF.Sign, bias=scl[:, S_PHG:S_PHG + 1],
                  deps=[eg1], tag=f"tg2_{c}")
        eg3 = tt(terng[cpar][:], terng[cpar][:], s2scr[:], ALU.add, deps=[eg2],
                 tag=f"tg3_{c}")
        eu1 = act(ternu[cpar][:], arf[:, 2048:4096], AF.Sign, bias=scl[:, S_NHU:S_NHU + 1],
                  deps=[elu, ev_tern_use["u"][cpar], eg2] + ev_su, tag=f"tu1_{c}")
        eu2 = act(s2scr[:], arf[:, 2048:4096], AF.Sign, bias=scl[:, S_PHU:S_PHU + 1],
                  deps=[eu1, eg3], tag=f"tu2_{c}")
        eu3 = tt(ternu[cpar][:], ternu[cpar][:], s2scr[:], ALU.add, deps=[eu2],
                 tag=f"tu3_{c}")
        aru[nmA] = eu2
        bk = 4 * (c % 2)
        lg = [None, None]
        for fc in range(16):
            for j in range(2):
                dep = [eg3, ev_xq[fc]]
                if fc == 0:
                    dep.append(ev_sil_rd.get((c - 2, j)))
                lg[j] = mm(gu_ps[bk + j][:], terng[cpar][:, 128 * fc:128 * (fc + 1)],
                           xq[fc][:, 512 * j:512 * (j + 1)], fc == 0, fc == 15,
                           deps=dep, tag=f"gmm{c}_{fc}_{j}")
        ev_gu[("g", c)] = lg[1]
        lu = [None, None]
        for fc in range(16):
            for j in range(2):
                dep = [eu3, ev_xq[fc]]
                if fc == 0:
                    dep.append(ev_zz_rd.get((c - 2, j)))
                lu[j] = mm(gu_ps[bk + 2 + j][:], ternu[cpar][:, 128 * fc:128 * (fc + 1)],
                           xq[fc][:, 512 * j:512 * (j + 1)], fc == 0, fc == 15,
                           deps=dep, tag=f"umm{c}_{fc}_{j}")
        ev_gu[("u", c)] = lu[1]
        ev_tern_use["g"][cpar] = lg[1]
        ev_tern_use["u"][cpar] = lu[1]
        ezz = []
        for j in range(2):
            es1 = act(sgt[:], gu_ps[bk + j][:], AF.Silu, scale=scl[:, S_HG:S_HG + 1],
                      deps=[lg[j], (ev_zst.get(c - 1)),
                            (ezz[-1] if ezz else None)] + ev_sg, tag=f"sil{c}_{j}")
            ev_sil_rd[(c, j)] = es1
            ez = tt(zt[:, 512 * j:512 * (j + 1)], sgt[:], gu_ps[bk + 2 + j][:],
                    ALU.mult, deps=[es1, lu[j]], tag=f"zmul{c}_{j}")
            ev_zz_rd[(c, j)] = ez
            ezz.append(ez)
        ev_zst[c] = dma("sync", g["z_hbm"][128 * c:128 * (c + 1), :], zt[:],
                        deps=ezz, tag=f"zst{c}")
    c3.close()
    cxq.close()

    czh = ExitStack()
    Ez = czh.enter_context
    zhA = sb_(Ez, "zhA", 128, 64 * 512, BF16)
    zhB = sb_(Ez, "zhB", 128, 64 * 512, BF16)
    c4 = ExitStack()
    E4 = c4.enter_context
    tscr2 = sb_(E4, "tscr2", 128, 1024, F32)
    z1q = sb_(E4, "z1q", 128, 1024, F32)
    z2q = sb_(E4, "z2q", 128, 1024, BF16)
    Mq2 = sb_(E4, "Mq2", 128, 512, F32)
    sB2 = sb_(E4, "sB2", 128, 512, F32)
    rB2 = sb_(E4, "rB2", 128, 512, BF16)
    f2_ps = [ps_(E4, "f2_ps0", 128, 512), ps_(E4, "f2_ps1", 128, 512)]
    c5 = ExitStack()
    E5 = c5.enter_context
    ternd = sb_(E5, "ternd", 128, 8192, BF16)
    dob = sb_(E5, "dob", 128, 512, BF16)
    dtb = sb_(E5, "dtb", 128, 512, BF16)
    dnt = sb_(E5, "dnt", 128, 512, BF16)
    xtd = sb_(E5, "xtd", 128, 512, F32)
    gbT2 = sb_(E5, "gbT2", 128, D, BF16)
    d_ps = [ps_(E5, "d_ps0", 128, 512), ps_(E5, "d_ps1", 128, 512)]
    dT_ps = ps_(E5, "dT_ps", 128, 512)
    d2_ps = ps_(E5, "d2_ps", 128, 512)

    ev_gbL = dma("sync", gbT2[:], g["gb_hbm"], deps=[ev_st_gb], tag="ld_gb2")
    ev_gb2 = add("dve", lambda h: h.tensor_single_scalar(
        gbT2[:], gbT2[:], scl[:, S_LAMD:S_LAMD + 1], ALU.mult),
        deps=[ev_gbL, elam], tag="gb_lam")

    zS = {0: zhA, 1: zhB}
    wdT3 = g["wdT"].rearrange("(fc p) o -> p fc o", p=128)

    def zsl(th_, c):
        return zS[th_][:, 512 * c:512 * (c + 1)]

    def ccb(col):
        return cst[:, col:col + 1].broadcast_to([128, 512])

    def ptt(out, a, b, op, deps=(), tag=""):
        return add("pool", lambda h, o=out, a=a, b=b, op=op:
                   h.tensor_tensor(o, a, b, op), deps=deps, tag=tag)

    use4 = {"tscr": None, "z1q": None, "z2q": None, "Mq2": None, "sB2": None,
            "rB2": None, "sprow": ev_sB + [erc], "dnt": None, "xtd": None,
            "d2": None}
    ev_u = {}
    ev_dn2 = {}

    def phase_load_h128(th_, extra=()):
        evl = {}
        evh = {}
        for c in range(64):
            evl[c] = dma("sync", zsl(th_, c),
                         g["z_hbm"][128 * c:128 * (c + 1), 512 * th_:512 * (th_ + 1)],
                         deps=[ev_zst[c]], tag=f"zl{th_}_{c}")
        for c in range(64):
            pj = c % 2
            em_ = mm(f2_ps[pj][:], h128[:], zsl(th_, c), True, True,
                     deps=[evl[c], evC[3], evh.get(c - 2)]
                     + (list(extra) if c < 2 else []), tag=f"f2m{th_}_{c}")
            evh[c] = act(zsl(th_, c), f2_ps[pj][:], AF.Identity, deps=[em_],
                         tag=f"f2e{th_}_{c}")
        return evh

    def phase_mix(th_, evh):
        last = None
        for st in range(6):
            hstep = 1 << st
            f = 512 * hstep
            vw = zS[th_][:].rearrange("p (g two f) -> p g two f", two=2, f=f)
            av = vw[:, :, 0, :]
            bv = vw[:, :, 1, :]
            dep = [last] if last else list(evh.values())
            e1m = tt(av, av, bv, ALU.add, deps=dep, tag=f"w2a{th_}_{st}")
            last = stt(bv, bv, -2.0, av, ALU.mult, ALU.add, deps=[e1m],
                       tag=f"w2b{th_}_{st}")
        return last

    def phase_amax_scales(th_, ev_mix):
        eM = None
        eN = None
        for p in range(32):
            a = zS[th_][:, 1024 * p:1024 * p + 512]
            b = zS[th_][:, 1024 * p + 512:1024 * (p + 1)]
            e1 = tt(z2q[:, 0:512], a, b, ALU.max,
                    deps=[ev_mix, use4["z2q"], eM], tag=f"q2x{th_}_{p}")
            e3 = tt(z2q[:, 512:1024], a, b, ALU.min, deps=[e1], tag=f"q2n{th_}_{p}")
            if p == 0:
                eM = tcp(Mq2[:], z2q[:, 0:512], deps=[e1, use4["Mq2"]],
                         tag=f"q2M{th_}_{p}")
                eN = tcp(sB2[:], z2q[:, 512:1024], deps=[e3, use4["sB2"]],
                         tag=f"q2N{th_}_{p}")
            else:
                eM = tt(Mq2[:], Mq2[:], z2q[:, 0:512], ALU.max, deps=[e1, eM],
                        tag=f"q2M{th_}_{p}")
                eN = tt(sB2[:], sB2[:], z2q[:, 512:1024], ALU.min, deps=[e3, eN],
                        tag=f"q2N{th_}_{p}")
        em1 = ts1(sB2[:], sB2[:], -1.0, ALU.mult, deps=[eN], tag=f"q2ng{th_}")
        eM = tt(Mq2[:], Mq2[:], sB2[:], ALU.max, deps=[eM, em1], tag=f"q2mm{th_}")
        use4["z2q"] = e1
        ev_r2 = []
        t2 = None
        for tb in range(4):
            t1 = mm(f2_ps[0][:, 0:128], Mq2[:, 128 * tb:128 * (tb + 1)], i128f[:],
                    True, True, deps=[eM, t2] + ([ev_r2[-1]] if ev_r2 else []),
                    tag=f"q2tp{th_}_{tb}")
            t2 = redmax(wacc[:, 6:7], f2_ps[0][:, 0:128], deps=[t1],
                        tag=f"q2rm{th_}_{tb}")
            t3 = mm(f2_ps[0][0:1, 128:256], wacc[:, 6:7], i128f[:], True, True,
                    deps=[t2], tag=f"q2rw{th_}_{tb}")
            ev_r2.append(act(sprow[:, 128 * tb:128 * (tb + 1)], f2_ps[0][0:1, 128:256],
                             AF.Identity, deps=[t3] + ([] if ev_r2 else use4["sprow"]),
                             tag=f"q2re{th_}_{tb}"))
        ec1 = ts2(sprow[0:1, 0:512], sprow[0:1, 0:512], scl[0:1, S_K2:S_K2 + 1], 1e-5,
                  ALU.mult, ALU.max, deps=ev_r2 + [ek2a], tag=f"q2cl{th_}")
        ec2 = recip(sprow[0:1, 512:1024], sprow[0:1, 0:512], deps=[ec1],
                    tag=f"q2rc{th_}")
        b1 = mm(f2_ps[0][:, 0:512], ones[0:1, :], sprow[0:1, 0:512], True, True,
                deps=[ec1, ev_r2[-1]], tag=f"q2bm{th_}")
        e_rb = act(rB2[:], f2_ps[0][:, 0:512], AF.Identity, deps=[b1, use4["rB2"]],
                   tag=f"q2be{th_}")
        b2 = mm(f2_ps[0][:, 0:512], ones[0:1, :], sprow[0:1, 512:1024], True, True,
                deps=[ec2, e_rb], tag=f"q2bm2{th_}")
        e_sb = act(sB2[:], f2_ps[0][:, 0:512], AF.Identity, deps=[b2, eM],
                   tag=f"q2be2{th_}")
        er1 = ts1(rB2[:], rB2[:], 1.0 / 127.0, ALU.mult, deps=[e_rb],
                  tag=f"q2rs{th_}")
        er2 = add("dve", lambda h: h.tensor_single_scalar(
            sB2[:], sB2[:], scl[:, S_K2_127:S_K2_127 + 1], ALU.mult),
            deps=[e_sb, ek2b], tag=f"q2ss{th_}")
        use4["sprow"] = [b1, b2, ec2]
        use4["Mq2"] = t1
        use4["sB2"] = er2
        return {"rB": er1, "sB": er2, "rows_last": ev_r2[-1],
                "ps_rd": [e_rb, e_sb, ev_r2[-1]]}

    def emit_quant_pair(th_, p, ev_mix, sc_):
        zp3 = zS[th_][:, 1024 * p:1024 * (p + 1)].rearrange(
            "p (two f) -> p two f", f=512)
        tp3 = tscr2[:].rearrange("p (two f) -> p two f", f=512)
        sbb = sB2[:].unsqueeze(1).broadcast_to([128, 2, 512])
        rbb = rB2[:].unsqueeze(1).broadcast_to([128, 2, 512])
        et = add("pool", lambda h: h.tensor_tensor(tp3, zp3, sbb, ALU.mult),
                 deps=[ev_mix, sc_["sB"], use4["tscr"]], tag=f"q2t{th_}_{p}")
        ez1 = act(z1q[:], tscr2[:], AF.Abs, deps=[et, use4["z1q"]],
                  tag=f"q2z1_{th_}_{p}")
        for ci, cf in enumerate(C_FOLDS):
            ez1 = act(z1q[:], z1q[:], AF.Abs, bias=cc(cf), deps=[ez1],
                      tag=f"q2f1_{th_}_{p}_{ci}")
        if p % 2 == 0:
            ez2 = act(z2q[:], tscr2[:], AF.Abs, bias=cc(C_NEGQ),
                      deps=[et, use4["z2q"]], tag=f"q2da_{th_}_{p}")
            for ci, cf in enumerate(C_FOLDS):
                ez2 = act(z2q[:], z2q[:], AF.Abs, bias=cc(cf), deps=[ez2],
                          tag=f"q2dg{th_}_{p}_{ci}")
        else:
            d0 = ts1(z2q[:], tscr2[:], -0.25, ALU.add,
                     deps=[et, use4["z2q"]], tag=f"q2d0_{th_}_{p}")
            ez2 = stt(z2q[:], z2q[:], -1.0, z2q[:], ALU.mult, ALU.max,
                      deps=[d0], tag=f"q2da_{th_}_{p}")
            for ci in range(8):
                vv = float(-(2.0 ** (6 - ci)) if ci < 7 else -0.5)
                f0 = ts1(z2q[:], z2q[:], vv, ALU.add, deps=[ez2],
                         tag=f"q2df{th_}_{p}_{ci}")
                ez2 = stt(z2q[:], z2q[:], -1.0, z2q[:], ALU.mult, ALU.max,
                          deps=[f0], tag=f"q2dg{th_}_{p}_{ci}")
        eg = ts1(z2q[:], z2q[:], 0.25, ALU.is_ge, deps=[ez2], tag=f"q2g{th_}_{p}")
        esg = ts2(z2q[:], z2q[:], 2.0, -1.0, ALU.mult, ALU.add, deps=[eg],
                  tag=f"q2sg{th_}_{p}")
        ed1 = ts2(z1q[:], z1q[:], -1.0, 0.5, ALU.mult, ALU.add, deps=[ez1],
                  tag=f"q2d1{th_}_{p}")
        ee = ptt(z1q[:], z1q[:], z2q[:], ALU.mult, deps=[ed1, esg],
                 tag=f"q2e{th_}_{p}")
        eq = ptt(tscr2[:], tscr2[:], z1q[:], ALU.subtract, deps=[ee],
                 tag=f"q2q{th_}_{p}")
        euq = add("pool", lambda h: h.tensor_tensor(zp3, tp3, rbb, ALU.mult),
                  deps=[eq, sc_["rB"]], tag=f"q2u{th_}_{p}")
        use4["tscr"] = euq
        use4["z1q"] = eq
        use4["z2q"] = ee
        ev_u[(th_, 2 * p)] = euq
        ev_u[(th_, 2 * p + 1)] = euq

    dstate = {"prev_mm": aru.get("last_pe"), "edb": {}, "edt": {}, "etr": {},
              "est": {}}

    def emit_down_ob(th_, ob):
        elA = dma("sync", arA[:].rearrange("p (fc o) -> p fc o", o=128),
                  wdT3[:, 0:32, 128 * ob:128 * (ob + 1)], deps=[aru["A"]],
                  tag=f"dldA{th_}_{ob}")
        elB = dma("sync", arB[:].rearrange("p (fc o) -> p fc o", o=128),
                  wdT3[:, 32:64, 128 * ob:128 * (ob + 1)], deps=[aru["B"]],
                  tag=f"dldB{th_}_{ob}")
        tern_evs = []
        for half, (ar_, el_) in enumerate(((arA, elA), (arB, elB))):
            sl = slice(4096 * half, 4096 * (half + 1))
            t1 = add("dve", lambda h, sl=sl, ar_=ar_: h.tensor_single_scalar(
                ternd[:, sl], ar_[:], scl[:, S_PHD:S_PHD + 1], ALU.is_ge),
                deps=[el_, dstate.get(("trd", th_, ob - 1))] + ev_sd,
                tag=f"dt1_{th_}_{ob}_{half}")
            t2 = add("dve", lambda h, ar_=ar_: h.tensor_single_scalar(
                ar_[:], ar_[:], scl[:, S_NHD:S_NHD + 1], ALU.is_le),
                deps=[t1], tag=f"dt2_{th_}_{ob}_{half}")
            t3 = tt(ternd[:, sl], ternd[:, sl], ar_[:], ALU.subtract, deps=[t2],
                    tag=f"dt3_{th_}_{ob}_{half}")
            tern_evs.append(t3)
            if half == 0:
                aru["A"] = t3
            else:
                aru["B"] = t3
        pb = d_ps[ob % 2]
        lmm = None
        for fc in range(64):
            dep = [tern_evs[fc // 32], ev_u[(th_, fc)]]
            if fc == 0:
                dep += [dstate["edb"].get((th_, ob - 2)) or dstate["edb"].get((th_ - 1, 14 + ob)),
                        dstate["prev_mm"]]
            lmm = mm(pb[:], ternd[:, 128 * fc:128 * (fc + 1)], zsl(th_, fc),
                     fc == 0, fc == 63, deps=dep, tag=f"dmm{th_}_{ob}_{fc}")
        dstate["prev_mm"] = lmm
        edb = act(dob[:], pb[:], AF.Identity,
                  deps=[lmm] + dstate["etr"].get(("prev",), []),
                  tag=f"dev{th_}_{ob}")
        dstate["edb"][(th_, ob)] = edb
        etr = []
        for j2 in range(4):
            etr.append(mm(dT_ps[:, 128 * j2:128 * (j2 + 1)],
                          dob[:, 128 * j2:128 * (j2 + 1)], i128b[:], True, True,
                          deps=[edb, ev_ib, dstate["edt"].get(("prev",))],
                          tag=f"dtr{th_}_{ob}_{j2}"))
        dstate["etr"][("prev",)] = etr
        edt = act(dtb[:], dT_ps[:], AF.Identity,
                  deps=etr + dstate["est"].get(("prev",), []),
                  tag=f"dte{th_}_{ob}")
        dstate["edt"][("prev",)] = edt
        ests = []
        for j2 in range(4):
            ests.append(dma("sync",
                            g["dn2_hbm"][512 * th_ + 128 * j2:512 * th_ + 128 * (j2 + 1),
                                         128 * ob:128 * (ob + 1)],
                            dtb[:, 128 * j2:128 * (j2 + 1)], deps=[edt],
                            tag=f"dst{th_}_{ob}_{j2}"))
        dstate["est"][("prev",)] = ests
        for j2 in range(4):
            ev_dn2[(th_, j2, ob)] = ests[j2]
        dstate[("trd", th_, ob)] = lmm

    def emit_out_step(th_, tql, qd):
        tqg = 4 * th_ + tql
        dl = dma("sync", dnt[:],
                 g["dn2_hbm"][512 * th_ + 128 * tql:512 * th_ + 128 * (tql + 1),
                              512 * qd:512 * (qd + 1)],
                 deps=[ev_dn2[(th_, tql, ob)] for ob in range(4 * qd, 4 * (qd + 1))]
                 + [use4["dnt"]], tag=f"onl{th_}_{tql}_{qd}")
        xl = dma("sync", xtd[:],
                 g["x1_hbm"][128 * tqg:128 * (tqg + 1), 512 * qd:512 * (qd + 1)],
                 deps=[ev_x1_st[tqg], use4["xtd"]], tag=f"oxl{th_}_{tql}_{qd}")
        e1 = tt(d2_ps[:], dnt[:], gbT2[:, 512 * qd:512 * (qd + 1)], ALU.mult,
                deps=[dl, ev_gb2, use4["d2"]], tag=f"ogb{th_}_{tql}_{qd}")
        e2 = tt(xtd[:], xtd[:], d2_ps[:], ALU.add, deps=[xl, e1],
                tag=f"ox2{th_}_{tql}_{qd}")
        est = dma("sync", g["out"][128 * tqg:128 * (tqg + 1), 512 * qd:512 * (qd + 1)],
                  xtd[:], deps=[e2], tag=f"ost{th_}_{tql}_{qd}")
        use4["dnt"] = e1
        use4["xtd"] = est
        use4["d2"] = e2

    evhA = phase_load_h128(0)
    mxA = phase_mix(0, evhA)
    scA = phase_amax_scales(0, mxA)
    for p in range(32):
        emit_quant_pair(0, p, mxA, scA)
    evhB = phase_load_h128(1, extra=scA["ps_rd"] + [evhA[62], evhA[63]])
    mxB = phase_mix(1, evhB)
    emit_down_ob(0, 0)
    emit_down_ob(0, 1)
    scB = phase_amax_scales(1, mxB)
    qa = 0
    for ob in range(2, 16):
        emit_down_ob(0, ob)
        for _ in range(3):
            if qa < 32:
                emit_quant_pair(1, qa, mxB, scB)
                qa += 1
    while qa < 32:
        emit_quant_pair(1, qa, mxB, scB)
        qa += 1
    outsteps = [(tql, qd) for tql in range(4) for qd in range(4)]
    oi = 0
    for ob in range(16):
        emit_down_ob(1, ob)
        if oi < len(outsteps):
            emit_out_step(0, *outsteps[oi])
            oi += 1
    while oi < len(outsteps):
        emit_out_step(0, *outsteps[oi])
        oi += 1
    for tql in range(4):
        for qd in range(4):
            emit_out_step(1, tql, qd)
    c5.close()
    c4.close()
    czh.close()


_cache = {}


def kernel(L0_attn_x, L0_attn_wq, L0_attn_wk, L0_attn_wv, L0_attn_wo,
           L0_attn_gq, L0_attn_gk, L0_ffn_w_gate, L0_ffn_w_up, L0_ffn_w_down,
           L0_c, L0_wmod):
    x = np.ascontiguousarray(np.asarray(L0_attn_x, dtype=np.float32))
    wq = np.ascontiguousarray(np.asarray(L0_attn_wq, np.float32))
    wk = np.ascontiguousarray(np.asarray(L0_attn_wk, np.float32))
    wv = np.ascontiguousarray(np.asarray(L0_attn_wv, np.float32))
    wo = np.ascontiguousarray(np.asarray(L0_attn_wo, np.float32))
    gq = np.ascontiguousarray(np.asarray(L0_attn_gq, np.float32).reshape(1, D))
    gk = np.ascontiguousarray(np.asarray(L0_attn_gk, np.float32).reshape(1, 512))
    wgT = np.ascontiguousarray(np.asarray(L0_ffn_w_gate, np.float32).T)
    wuT = np.ascontiguousarray(np.asarray(L0_ffn_w_up, np.float32).T)
    wdT = np.ascontiguousarray(np.asarray(L0_ffn_w_down, np.float32).T)
    cvec = np.ascontiguousarray(np.asarray(L0_c, np.float32))
    wmod = np.ascontiguousarray(np.asarray(L0_wmod, np.float32))

    cst, ones_h, i128f_h, h128_h, diag_h, ropeC, ropeS = _host_consts()
    if "nc" not in _cache:
        _cache["nc"] = build_program()
    nc = _cache["nc"]

    in_maps = []
    metas = []
    for core in range(8):
        b = core // 2
        par = core % 2
        kv, qtl, dead = _parity_layout(par)
        xb = x[b]
        x_kv = np.concatenate([xb[128 * s:128 * (s + 1)] for s in kv], 0)
        x_q = np.concatenate([xb[128 * s:128 * (s + 1)] for s in qtl], 0)
        rC_kv = np.concatenate([ropeC[128 * s:128 * (s + 1)] for s in kv], 0)
        rS_kv = np.concatenate([ropeS[128 * s:128 * (s + 1)] for s in kv], 0)
        rC_q = np.concatenate([ropeC[128 * s:128 * (s + 1)] for s in qtl], 0)
        rS_q = np.concatenate([ropeS[128 * s:128 * (s + 1)] for s in qtl], 0)
        im = dict(
            x_kv=np.ascontiguousarray(x_kv), x_q=np.ascontiguousarray(x_q),
            ropeC_kv=np.ascontiguousarray(rC_kv), ropeS_kv=np.ascontiguousarray(rS_kv),
            ropeC_q=np.ascontiguousarray(rC_q), ropeS_q=np.ascontiguousarray(rS_q),
            deadrow=np.ascontiguousarray(dead.reshape(1, 32)),
            wq=wq, wk=wk, wv=wv, wo=wo, wgT=wgT, wuT=wuT, wdT=wdT, wmod=wmod,
            c_vec=np.ascontiguousarray(cvec[b:b + 1]), gq_row=gq, gk_row=gk,
            cstab=cst, ones=ones_h, i128f=i128f_h, h128=h128_h, diagmask=diag_h,
        )
        in_maps.append(im)
        metas.append((b, qtl))

    res = run_bass_kernel_spmd(nc, in_maps, core_ids=list(range(8)))
    out = np.zeros((B, T, D), np.float32)
    for core in range(8):
        b, qtl = metas[core]
        o = res.results[core]["out"]
        for i, s in enumerate(qtl):
            out[b, 128 * s:128 * (s + 1), :] = o[128 * i:128 * (i + 1), :]
    return out
